# version 02 (final)
# baseline (speedup 1.0000x reference)
import numpy as np
import ml_dtypes
import concourse.bass as bass
import concourse.mybir as mybir
from concourse.bass_utils import run_bass_kernel_spmd

F32 = mybir.dt.float32
BF16 = mybir.dt.bfloat16
AF = mybir.ActivationFunctionType
ALU = mybir.AluOpType
AX = mybir.AxisListType

B, S, D = 4, 8192, 1024
NCORES = 8
ROWS = B * S // NCORES
NCH = ROWS // 128

BF = ml_dtypes.bfloat16
IDENT = np.eye(128, dtype=BF)

CONST_VALS = [1e-5, 1.0 / 1024.0, -0.5, 1e-8, 2.0, -1.0, 0.5,
              1.0 / 4096.0, -1.0 / 1024.0, 1024.0, 1024.0 ** -3, 2048.0,
              2048.0 ** -3, 4096.0]
CONSTS_NP = np.asarray([CONST_VALS], dtype=np.float32)


def decl_consts(nc, e):
    cd = nc.dram_tensor("consts", [1, len(CONST_VALS)], F32,
                        kind="ExternalInput").ap()
    cs = e(nc.sbuf_tensor("consts_sb", [128, len(CONST_VALS)], F32))
    for i, v in enumerate(CONST_VALS):
        nc.const_aps.register(F32, float(v), cs[:, i:i + 1])
        if float(np.float32(v)) != float(v):
            nc.const_aps.register(F32, float(np.float32(v)), cs[:, i:i + 1])
    def load(eng, sem):
        return eng.dma_start(out=cs[:], in_=cd.partition_broadcast(128)).then_inc(sem, 16)
    return cd, cs, load


def new_nc():
    return bass.Bass("TRN2", target_bir_lowering=False, debug=False,
                     detect_race_conditions=False)


def shard_rows(x):
    x2 = np.ascontiguousarray(x).reshape(B * S, -1)
    return [np.ascontiguousarray(x2[c * ROWS:(c + 1) * ROWS]) for c in range(NCORES)]


def unshard_rows(parts, d=D):
    return np.concatenate(parts, axis=0).reshape(B, S, d)



def emit_ln_stats(s, y_ap, junk_ap, stats_ap):
    s.activation(out=junk_ap, in_=y_ap, func=AF.Copy, scale=-1.0,
                 accum_out=stats_ap[:, 0:1])
    s.activation(out=stats_ap[:, 1:2], in_=stats_ap[:, 0:1], func=AF.Copy,
                 scale=1.0 / 1024.0)
    s.activation(out=junk_ap, in_=y_ap, func=AF.Square, bias=stats_ap[:, 0:1],
                 scale=1024.0, accum_out=stats_ap[:, 2:3])
    s.activation(out=stats_ap[:, 3:4], in_=stats_ap[:, 2:3], func=AF.Ln,
                 bias=1e-5, scale=1024.0 ** -3)
    s.activation(out=stats_ap[:, 3:4], in_=stats_ap[:, 3:4], func=AF.Exp,
                 scale=-0.5)
    return s.drain()



def build_l0out():
    nc = new_nc()
    attn = nc.dram_tensor("attn", [ROWS, D], BF16, kind="ExternalInput").ap()
    x = nc.dram_tensor("x", [ROWS, D], F32, kind="ExternalInput").ap()
    wo = nc.dram_tensor("wo", [D, D], F32, kind="ExternalInput").ap()
    bo = nc.dram_tensor("bo", [1, D], F32, kind="ExternalInput").ap()
    ident = nc.dram_tensor("ident", [128, 128], BF16, kind="ExternalInput").ap()
    x1 = nc.dram_tensor("x1", [ROWS, D], F32, kind="ExternalOutput").ap()
    x1b = nc.dram_tensor("x1b", [ROWS, D], BF16, kind="ExternalOutput").ap()

    from contextlib import ExitStack
    with ExitStack() as ctx:
        e = ctx.enter_context
        cd, cs, load_consts = decl_consts(nc, e)
        wo_sb = e(nc.sbuf_tensor("wo_sb", [128, 8 * D], BF16))
        bo_sb = e(nc.sbuf_tensor("bo_sb", [1, D], BF16))
        ones1 = e(nc.sbuf_tensor("ones1", [1, 128], BF16))
        id_sb = e(nc.sbuf_tensor("id_sb", [128, 128], BF16))
        a_bf = [e(nc.sbuf_tensor(f"a_bf{j}", [128, D], BF16)) for j in range(2)]
        x_bf = [e(nc.sbuf_tensor(f"x_bf{j}", [128, D], BF16)) for j in range(2)]
        at_sb = [e(nc.sbuf_tensor(f"at_sb{j}", [128, D], BF16)) for j in range(2)]
        x1_f = [e(nc.sbuf_tensor(f"x1_f{j}", [128, D], F32)) for j in range(2)]
        sq_f = e(nc.sbuf_tensor("sq_f", [128, D], F32))
        stats = [e(nc.sbuf_tensor(f"stats{j}", [128, 4], F32)) for j in range(2)]
        stats2 = [e(nc.sbuf_tensor(f"stats2{j}", [128, 4], F32)) for j in range(2)]
        ps_tr = [e(nc.psum_tensor(f"ps_tr{j}", [128, 8 * 128], BF16)) for j in range(2)]
        ps_o = [e(nc.psum_tensor(f"ps_o{j}", [128, D], F32)) for j in range(2)]
        ps_fence = e(nc.psum_tensor("ps_fence", [1, 256], F32))

        ld_a = e(nc.semaphore("ld_a"))
        ld_xb = e(nc.semaphore("ld_xb"))
        ld_s = e(nc.semaphore("ld_s"))
        wl = e(nc.semaphore("wl"))
        pe = e(nc.semaphore("pe"))
        dv = e(nc.semaphore("dv"))
        sc = e(nc.semaphore("sc"))
        st = e(nc.semaphore("st"))
        sb = e(nc.semaphore("sb"))
        block = e(nc.Block())

        @block.sync
        def _(sync):
            sync.dma_start(out=id_sb[:], in_=ident[:]).then_inc(ld_s, 16)

        @block.gpsimd
        def _(g):
            for c_ in range(8):
                g.dma_start(out=wo_sb[:, D * c_:D * (c_ + 1)],
                            in_=wo[128 * c_:128 * (c_ + 1), :]).then_inc(wl, 16)
            g.dma_start(out=bo_sb[:], in_=bo[:]).then_inc(wl, 16)
            load_consts(g, wl)
            for i in range(NCH):
                j = i % 2
                if i >= 2:
                    g.wait_ge(pe, 2 * (i - 2) + 1)
                g.dma_start(out=a_bf[j][:],
                            in_=attn[128 * i:128 * (i + 1), :]).then_inc(ld_a, 16)
                if i >= 2:
                    g.wait_ge(pe, 2 * (i - 2) + 2)
                g.dma_start(out=x_bf[j][:],
                            in_=x[128 * i:128 * (i + 1), :]).then_inc(ld_xb, 16)
                g.wait_ge(sc, i + 1)
                g.dma_start(out=stats2[j][:], in_=stats[j][:]).then_inc(sb, 16)
                if i >= 1:
                    g.wait_ge(dv, 1 + 2 * (i - 1) + 2)
                    g.dma_start(out=x1[128 * (i - 1):128 * i, :],
                                in_=x1_f[(i - 1) % 2][:]).then_inc(st, 16)
                    g.dma_start(out=x1b[128 * (i - 1):128 * i, :],
                                in_=x1_f[(i - 1) % 2][:]).then_inc(st, 16)
            g.wait_ge(dv, 1 + 2 * (NCH - 1) + 2)
            g.dma_start(out=x1[128 * (NCH - 1):128 * NCH, :],
                        in_=x1_f[(NCH - 1) % 2][:]).then_inc(st, 16)
            g.dma_start(out=x1b[128 * (NCH - 1):128 * NCH, :],
                        in_=x1_f[(NCH - 1) % 2][:]).then_inc(st, 16)
            g.wait_ge(st, 32 * NCH)

        @block.tensor
        def _(t):
            t.wait_ge(wl, 160)
            t.wait_ge(ld_s, 16)
            t.wait_ge(dv, 1)
            for i in range(NCH):
                j = i % 2
                t.wait_ge(ld_a, 16 * (i + 1))
                if i >= 2:
                    t.wait_ge(dv, 1 + 2 * (i - 2) + 1)
                for k in range(8):
                    t.transpose(ps_tr[j][:, 128 * k:128 * (k + 1)],
                                a_bf[j][:, 128 * k:128 * (k + 1)], id_sb[:])
                t.drain().then_inc(pe, 1)
                t.wait_ge(dv, 1 + 2 * i + 1)
                t.wait_ge(ld_xb, 16 * (i + 1))
                for n in range(2):
                    for k in range(8):
                        t.matmul(ps_o[j][:, 512 * n:512 * (n + 1)],
                                 at_sb[j][:, 128 * k:128 * (k + 1)],
                                 wo_sb[:, D * k + 512 * n:D * k + 512 * (n + 1)],
                                 start=(k == 0), stop=False)
                    t.matmul(ps_o[j][:, 512 * n:512 * (n + 1)],
                             ones1[:], bo_sb[:, 512 * n:512 * (n + 1)],
                             start=False, stop=False)
                    t.matmul(ps_o[j][:, 512 * n:512 * (n + 1)],
                             id_sb[:], x_bf[j][:, 512 * n:512 * (n + 1)],
                             start=False, stop=True)
                t.drain().then_inc(pe, 1)

        @block.vector
        def _(v):
            v.memset(ones1[:], 1.0).then_inc(dv, 1)
            for i in range(NCH):
                j = i % 2
                v.wait_ge(pe, 2 * i + 1)
                v.tensor_copy(at_sb[j][:], ps_tr[j][:])
                v.drain().then_inc(dv, 1)
                v.wait_ge(sb, 16 * (i + 1))
                if i >= 2:
                    v.wait_ge(st, 32 * (i - 1))
                v.tensor_scalar(out=x1_f[j][:], in0=ps_o[j][:],
                                scalar1=stats2[j][:, 1:2],
                                scalar2=stats2[j][:, 3:4],
                                op0=ALU.add, op1=ALU.mult)
                v.drain().then_inc(dv, 1)

        @block.scalar
        def _(s):
            s.wait_ge(wl, 160)
            for i in range(NCH):
                j = i % 2
                s.wait_ge(pe, 2 * i + 2)
                if i >= 2:
                    s.wait_ge(dv, 1 + 2 * (i - 2) + 2)
                emit_ln_stats(s, ps_o[j][:], sq_f[:], stats[j]).then_inc(sc, 1)
    return nc


def run_l0out(attn_full, x_full, Wo, bo):
    nc = build_l0out()
    attn_sh = shard_rows(attn_full)
    x_sh = shard_rows(x_full)
    wo_np = np.ascontiguousarray(Wo, dtype=np.float32)
    bo_np = np.ascontiguousarray(bo, dtype=np.float32).reshape(1, D)
    in_maps = [{"attn": attn_sh[c], "x": x_sh[c], "wo": wo_np, "bo": bo_np,
                "ident": IDENT, "consts": CONSTS_NP} for c in range(NCORES)]
    res = run_bass_kernel_spmd(nc, in_maps, core_ids=list(range(NCORES)))
    return (unshard_rows([res.results[c]["x1"] for c in range(NCORES)]),
            [res.results[c]["x1b"] for c in range(NCORES)])



NST = 4
TST = 1024
NSUB = 8
NHC = 32
NE = 8


def emit_coef_l0(v, E8, sumcol, mcol, mask8, coef_blk, spac, cs):
    v.reduce_max(mcol[:, 0:1], E8, axis=AX.X)
    v.reciprocal(mcol[:, 1:2], sumcol)
    v.tensor_copy(spac[:], cs[0:1, :])
    v.tensor_copy(spac[:], cs[0:1, :])
    v.tensor_tensor(out=mask8[:], in0=E8,
                    in1=mcol[:, 0:1].broadcast_to([128, 8]), op=ALU.is_equal)
    v.scalar_tensor_tensor(out=mcol[:, 2:3], in0=mcol[:, 0:1], scalar=2.0,
                           in1=mcol[:, 1:2], op0=ALU.mult, op1=ALU.mult)
    v.tensor_copy(spac[:], cs[0:1, :])
    v.tensor_copy(spac[:], cs[0:1, :])
    return v.tensor_tensor(out=coef_blk, in0=mask8[:],
                           in1=mcol[:, 2:3].broadcast_to([128, 8]), op=ALU.mult)


def emit_coef_l1(v, E8, mcol, mask8, num8, coef_blk, spac, cs):
    v.reduce_max(mcol[:, 0:1], E8, axis=AX.X)
    v.tensor_copy(spac[:], cs[0:1, :])
    v.tensor_copy(spac[:], cs[0:1, :])
    v.tensor_tensor(out=mask8[:], in0=E8,
                    in1=mcol[:, 0:1].broadcast_to([128, 8]), op=ALU.is_equal)
    v.tensor_scalar(out=num8[:], in0=mask8[:], scalar1=-1.0, scalar2=1.0,
                    op0=ALU.mult, op1=ALU.add)
    v.tensor_mul(num8[:], num8[:], E8)
    v.reduce_max(mcol[:, 1:2], num8[:], axis=AX.X)
    v.tensor_copy(spac[:], cs[0:1, :])
    v.tensor_copy(spac[:], cs[0:1, :])
    v.tensor_tensor(out=num8[:], in0=num8[:],
                    in1=mcol[:, 1:2].broadcast_to([128, 8]), op=ALU.is_equal)
    v.tensor_add(num8[:], num8[:], mask8[:])
    v.tensor_mul(num8[:], num8[:], E8)
    v.tensor_tensor(out=mcol[:, 2:3], in0=mcol[:, 0:1], in1=mcol[:, 1:2],
                    op=ALU.add)
    v.tensor_copy(spac[:], cs[0:1, :])
    v.tensor_copy(spac[:], cs[0:1, :])
    v.reciprocal(mcol[:, 3:4], mcol[:, 2:3])
    v.tensor_copy(spac[:], cs[0:1, :])
    v.tensor_copy(spac[:], cs[0:1, :])
    return v.tensor_tensor(out=coef_blk, in0=num8[:],
                           in1=mcol[:, 3:4].broadcast_to([128, 8]), op=ALU.mult)


def emit_coef_batch(v, l0, rst2, mc32, mask64, num64, coef):
    def E8(ch):
        return rst2[:, ch * 9:ch * 9 + 8]
    def mc(ch, k):
        return mc32[:, ch * 4 + k:ch * 4 + k + 1]
    def mk(ch):
        return mask64[:, ch * 8:(ch + 1) * 8]
    def nm(ch):
        return num64[:, ch * 8:(ch + 1) * 8]
    for ch in range(8):
        v.reduce_max(mc(ch, 0), E8(ch), axis=AX.X)
    if l0:
        for ch in range(8):
            v.reciprocal(mc(ch, 1), rst2[:, ch * 9 + 8:ch * 9 + 9])
        for ch in range(8):
            v.tensor_tensor(out=mk(ch), in0=E8(ch),
                            in1=mc(ch, 0).broadcast_to([128, 8]),
                            op=ALU.is_equal)
        for ch in range(8):
            v.scalar_tensor_tensor(out=mc(ch, 2), in0=mc(ch, 0), scalar=2.0,
                                   in1=mc(ch, 1), op0=ALU.mult, op1=ALU.mult)
        for ch in range(8):
            ins = v.tensor_tensor(out=coef[:, ch * 8:(ch + 1) * 8], in0=mk(ch),
                                  in1=mc(ch, 2).broadcast_to([128, 8]),
                                  op=ALU.mult)
        return ins
    for ch in range(8):
        v.tensor_tensor(out=mk(ch), in0=E8(ch),
                        in1=mc(ch, 0).broadcast_to([128, 8]), op=ALU.is_equal)
    for ch in range(8):
        v.tensor_scalar(out=nm(ch), in0=mk(ch), scalar1=-1.0, scalar2=1.0,
                        op0=ALU.mult, op1=ALU.add)
    for ch in range(8):
        v.tensor_mul(nm(ch), nm(ch), E8(ch))
    for ch in range(8):
        v.reduce_max(mc(ch, 1), nm(ch), axis=AX.X)
    for ch in range(8):
        v.tensor_tensor(out=nm(ch), in0=nm(ch),
                        in1=mc(ch, 1).broadcast_to([128, 8]), op=ALU.is_equal)
    for ch in range(8):
        v.tensor_add(nm(ch), nm(ch), mk(ch))
    for ch in range(8):
        v.tensor_mul(nm(ch), nm(ch), E8(ch))
    for ch in range(8):
        v.tensor_tensor(out=mc(ch, 2), in0=mc(ch, 0), in1=mc(ch, 1),
                        op=ALU.add)
    for ch in range(8):
        v.reciprocal(mc(ch, 3), mc(ch, 2))
    for ch in range(8):
        ins = v.tensor_tensor(out=coef[:, ch * 8:(ch + 1) * 8], in0=nm(ch),
                              in1=mc(ch, 3).broadcast_to([128, 8]), op=ALU.mult)
    return ins


def build_moe(mode, dbg_fixed_coef=None, dbg_coef=False):
    l0 = (mode == 'l0')
    NHCm = NHC if l0 else 16
    nc = new_nc()
    xb = nc.dram_tensor("xb", [ROWS, D], BF16, kind="ExternalInput").ap()
    wr = nc.dram_tensor("wr", [D, 8], F32, kind="ExternalInput").ap()
    wu = nc.dram_tensor("wu", [NE, NHCm, 128, 1024], F32, kind="ExternalInput").ap()
    if l0:
        wg = nc.dram_tensor("wg", [NE, NHCm, 128, 1024], F32, kind="ExternalInput").ap()
    wd = nc.dram_tensor("wd", [NE, NHCm, 128, 1024], F32, kind="ExternalInput").ap()
    if not l0:
        bgD = nc.dram_tensor("bg", [1, 8], F32, kind="ExternalInput").ap()
        b1D = nc.dram_tensor("b1", [NE, 128, NHCm], F32, kind="ExternalInput").ap()
        b2D = nc.dram_tensor("b2", [NE, 1, D], F32, kind="ExternalInput").ap()
    ident = nc.dram_tensor("ident", [128, 128], BF16, kind="ExternalInput").ap()
    out_f = nc.dram_tensor("out_f", [ROWS, D], F32, kind="ExternalOutput").ap()
    out_b = nc.dram_tensor("out_b", [ROWS, D], BF16, kind="ExternalOutput").ap()
    if dbg_coef:
        coef_o = nc.dram_tensor("coef_o", [NST, 128, NSUB * 8], F32,
                                kind="ExternalOutput").ap()
        rst_o = nc.dram_tensor("rst_o", [NST, 8, 128, 9], F32,
                               kind="ExternalOutput").ap()

    cnt = {k: 0 for k in ("pe", "ac", "dv", "tD", "aL", "wU", "wD", "rB",
                          "sB", "stO")}
    M = {}

    def mark(sem, key, n=1):
        cnt[sem] += n
        M[(sem,) + key] = cnt[sem]

    if not l0:
        mark('dv', ('ones',))
    for st in range(NST):
        for c in range(8):
            mark('tD', ('x', st, c), 16)
        for sub in range(NSUB):
            mark('aL', ('a', st, sub), 16)
        for ch in range(8):
            mark('pe', ('r', st, ch))
            mark('ac', ('rx', st, ch))
            mark('rB', ('rb', st, ch), 16)
        mark('dv', ('cfB', st))
        for e in range(NE):
            for hc in range(NHCm):
                mark('wU', ('wu', st, e, hc), 32 if l0 else 16)
                if (not l0) and hc == 0:
                    mark('wD', ('b', st, e), 32)
                mark('wD', ('wd', st, e, hc), 16)
                mark('pe', ('ug', st, e, hc))
                if l0:
                    mark('ac', ('g', st, e, hc))
                mark('dv', ('h', st, e, hc))
            for sub in range(NSUB):
                mark('pe', ('dn', st, e, sub))
                if (not l0) or e < 7:
                    mark('dv', ('acc', st, e, sub))
        for sub in range(NSUB):
            if not l0:
                mark('pe', ('li', st, sub))
            mark('ac', ('ln', st, sub))
            mark('sB', ('sb', st, sub), 16)
            mark('dv', ('ap', st, sub))
            mark('stO', ('o', st, sub), 32)

    def lin_of(st, e, hc):
        return (st * NE + e) * NHCm + hc

    def unlin(p):
        return (p // (NE * NHCm), (p // NHCm) % NE, p % NHCm)

    from contextlib import ExitStack
    with ExitStack() as ctx:
        e_ = ctx.enter_context
        cd, cs, load_consts = decl_consts(nc, e_)
        id_sb = e_(nc.sbuf_tensor("id_sb", [128, 128], BF16))
        wr_sb = e_(nc.sbuf_tensor("wr_sb", [128, 8 * 8], BF16))
        xT = e_(nc.sbuf_tensor("xT", [128, 8 * TST], BF16))
        hT = e_(nc.sbuf_tensor("hT", [128, NHCm * TST], BF16))
        wd_sb = e_(nc.sbuf_tensor("wd_sb", [128, NHCm * D], BF16))
        wu_b = [e_(nc.sbuf_tensor(f"wu_b{j}", [128, D], BF16)) for j in range(2)]
        if l0:
            wg_b = [e_(nc.sbuf_tensor(f"wg_b{j}", [128, D], BF16)) for j in range(2)]
            g_sb = [e_(nc.sbuf_tensor(f"g_sb{j}", [128, TST], F32)) for j in range(2)]
        else:
            ones1 = e_(nc.sbuf_tensor("ones1", [1, 128], BF16))
            bg_sb = e_(nc.sbuf_tensor("bg_sb", [1, 8], BF16))
            b1_sb = [e_(nc.sbuf_tensor(f"b1_sb{j}", [128, NHCm], F32)) for j in range(2)]
            b2_sb = [e_(nc.sbuf_tensor(f"b2_sb{j}", [1, D], BF16)) for j in range(2)]
        acc = e_(nc.sbuf_tensor("acc", [128, NSUB * D], BF16))
        x2f = [e_(nc.sbuf_tensor(f"x2f{j}", [128, D], F32)) for j in range(2)]
        sq_f = e_(nc.sbuf_tensor("sq_f", [128, D], BF16))
        coef = e_(nc.sbuf_tensor("coef", [128, NSUB * 8], F32))
        rstat = [e_(nc.sbuf_tensor(f"rstat{j}", [128, 9], F32)) for j in range(2)]
        rst2 = e_(nc.sbuf_tensor("rst2", [128, 8 * 9], F32))
        stats = [e_(nc.sbuf_tensor(f"stats{j}", [128, 4], F32)) for j in range(2)]
        stats2 = [e_(nc.sbuf_tensor(f"stats2{j}", [128, 4], F32)) for j in range(2)]
        mc32 = e_(nc.sbuf_tensor("mc32", [128, 32], F32))
        mask64 = e_(nc.sbuf_tensor("mask64", [128, 64], F32))
        num64 = e_(nc.sbuf_tensor("num64", [128, 64], F32))
        spac = e_(nc.sbuf_tensor("spac", [1, len(CONST_VALS)], F32))
        psA = e_(nc.psum_tensor("psA", [128, D], F32))
        if l0:
            psB = e_(nc.psum_tensor("psB", [128, D], F32))
        ps_d = e_(nc.psum_tensor("ps_d", [128, D], F32))
        ps_r = e_(nc.psum_tensor("ps_r", [128, 512], F32))

        wI = e_(nc.semaphore("wI"))
        wS = e_(nc.semaphore("wS"))
        tD = e_(nc.semaphore("tD"))
        aL = e_(nc.semaphore("aL"))
        wU = e_(nc.semaphore("wU"))
        wD = e_(nc.semaphore("wD"))
        rB = e_(nc.semaphore("rB"))
        sB = e_(nc.semaphore("sB"))
        pe = e_(nc.semaphore("pe"))
        ac = e_(nc.semaphore("ac"))
        dv = e_(nc.semaphore("dv"))
        stO = e_(nc.semaphore("stO"))
        dbgS = e_(nc.semaphore("dbgS")) if dbg_coef else None
        block = e_(nc.Block())

        @block.sync
        def _(sync):
            sync.dma_start(out=id_sb[:], in_=ident[:]).then_inc(wS, 16)
            for st in range(NST):
                if st >= 1:
                    sync.wait_ge(pe, M[('pe', 'ug', st - 1, NE - 1, NHCm - 1)])
                for c in range(8):
                    sync.dma_start(
                        out=xT[:, c * TST:(c + 1) * TST],
                        in_=xb[st * TST:(st + 1) * TST, c * 128:(c + 1) * 128],
                        transpose=True).then_inc(tD, 16)

        @block.gpsimd
        def _(g):
            g.dma_start(out=wr_sb.ap().rearrange("p (c n) -> p c n", c=8),
                        in_=wr.rearrange("(c p) n -> p c n", p=128)).then_inc(wI, 16)
            if not l0:
                g.dma_start(out=bg_sb[:], in_=bgD[:]).then_inc(wI, 16)
            load_consts(g, wI)
            for st in range(NST):
                for sub in range(NSUB):
                    if st >= 1:
                        g.wait_ge(pe, M[('pe', 'dn', st - 1, 7, sub)] if l0
                                  else M[('pe', 'li', st - 1, sub)])
                    g.dma_start(out=acc[:, sub * D:(sub + 1) * D],
                                in_=xb[st * TST + sub * 128:
                                       st * TST + (sub + 1) * 128, :]
                                ).then_inc(aL, 16)
                for ch in range(8):
                    g.wait_ge(ac, M[('ac', 'rx', st, ch)])
                    if ch == 0 and st >= 1:
                        g.wait_ge(dv, M[('dv', 'cfB', st - 1)])
                    g.dma_start(out=rst2[:, ch * 9:(ch + 1) * 9],
                                in_=rstat[ch % 2][:]).then_inc(rB, 16)
                    if dbg_coef:
                        g.dma_start(out=rst_o[st, ch],
                                    in_=rst2[:, ch * 9:(ch + 1) * 9]
                                    ).then_inc(dbgS, 16)
                if dbg_coef:
                    g.wait_ge(dv, M[('dv', 'cfB', st)])
                    g.dma_start(out=coef_o[st], in_=coef[:]).then_inc(dbgS, 16)
                for e in range(NE):
                    ge = st * NE + e
                    for hc in range(NHCm):
                        lin = lin_of(st, e, hc)
                        if lin >= 2:
                            g.wait_ge(pe, M[('pe', 'ug') + unlin(lin - 2)])
                        g.dma_start(out=wu_b[lin % 2][:],
                                    in_=wu[e, hc]).then_inc(wU, 16)
                        if l0:
                            g.dma_start(out=wg_b[lin % 2][:],
                                        in_=wg[e, hc]).then_inc(wU, 16)
                        if (not l0) and hc == 0:
                            if ge >= 2:
                                pge = ge - 2
                                g.wait_ge(dv, M[('dv', 'h', pge // NE, pge % NE,
                                                 NHCm - 1)])
                                g.wait_ge(pe, M[('pe', 'dn', pge // NE, pge % NE,
                                                 NSUB - 1)])
                            g.dma_start(out=b1_sb[e % 2][:],
                                        in_=b1D[e]).then_inc(wD, 16)
                            g.dma_start(out=b2_sb[e % 2][:],
                                        in_=b2D[e]).then_inc(wD, 16)
                        if hc == 0 and ge >= 1:
                            pge = ge - 1
                            g.wait_ge(pe, M[('pe', 'dn', pge // NE, pge % NE,
                                             NSUB - 1)])
                        g.dma_start(out=wd_sb[:, hc * D:(hc + 1) * D],
                                    in_=wd[e, hc]).then_inc(wD, 16)
                for sub in range(NSUB):
                    gs = st * NSUB + sub
                    g.wait_ge(ac, M[('ac', 'ln', st, sub)])
                    if gs >= 2:
                        g.wait_ge(dv, M[('dv', 'ap', (gs - 2) // NSUB,
                                         (gs - 2) % NSUB)])
                    g.dma_start(out=stats2[sub % 2][:],
                                in_=stats[sub % 2][:]).then_inc(sB, 16)
                    g.wait_ge(dv, M[('dv', 'ap', st, sub)])
                    r0 = st * TST + sub * 128
                    g.dma_start(out=out_f[r0:r0 + 128, :],
                                in_=x2f[sub % 2][:]).then_inc(stO, 16)
                    g.dma_start(out=out_b[r0:r0 + 128, :],
                                in_=x2f[sub % 2][:]).then_inc(stO, 16)
            g.wait_ge(stO, cnt['stO'])
            if dbg_coef:
                g.wait_ge(dbgS, 16 * NST * 9)

        @block.tensor
        def _(t):
            t.wait_ge(wI, 32 if l0 else 48)
            t.wait_ge(wS, 16)
            if not l0:
                t.wait_ge(dv, 1)
            for st in range(NST):
                t.wait_ge(tD, M[('tD', 'x', st, 7)])
                t.wait_ge(aL, M[('aL', 'a', st, NSUB - 1)])
                for ch in range(8):
                    if ch == 0 and st >= 1:
                        t.wait_ge(ac, M[('ac', 'rx', st - 1, 7)])
                    if ch >= 1:
                        t.wait_ge(ac, M[('ac', 'rx', st, ch - 1)])
                    for c in range(8):
                        t.matmul(ps_r[0:128, 0:8],
                                 xT[:, c * TST + ch * 128:c * TST + (ch + 1) * 128],
                                 wr_sb[:, c * 8:(c + 1) * 8],
                                 start=(c == 0), stop=(l0 and c == 7))
                    if not l0:
                        t.matmul(ps_r[0:128, 0:8], ones1[:], bg_sb[:],
                                 start=False, stop=True)
                    t.drain().then_inc(pe, 1)
                for e in range(NE):
                    for hc in range(NHCm):
                        lin = lin_of(st, e, hc)
                        t.wait_ge(wU, M[('wU', 'wu', st, e, hc)])
                        if lin >= 1:
                            pk = unlin(lin - 1)
                            t.wait_ge(dv, M[('dv', 'h') + pk])
                        for tb in range(2):
                            for c in range(8):
                                t.matmul(psA[:, tb * 512:(tb + 1) * 512],
                                         wu_b[lin % 2][:, c * 128:(c + 1) * 128],
                                         xT[:, c * TST + tb * 512:
                                            c * TST + (tb + 1) * 512],
                                         start=(c == 0), stop=(c == 7))
                        if l0:
                            if lin >= 1:
                                t.wait_ge(ac, M[('ac', 'g') + unlin(lin - 1)])
                            for tb in range(2):
                                for c in range(8):
                                    t.matmul(psB[:, tb * 512:(tb + 1) * 512],
                                             wg_b[lin % 2][:, c * 128:(c + 1) * 128],
                                             xT[:, c * TST + tb * 512:
                                                c * TST + (tb + 1) * 512],
                                             start=(c == 0), stop=(c == 7))
                        t.drain().then_inc(pe, 1)
                    t.wait_ge(wD, M[('wD', 'wd', st, e, NHCm - 1)])
                    t.wait_ge(dv, M[('dv', 'h', st, e, NHCm - 1)])
                    for sub in range(NSUB):
                        if l0 and e == 7:
                            if sub == 0:
                                t.wait_ge(dv, M[('dv', 'acc', st, 6, NSUB - 1)])
                            else:
                                t.wait_ge(ac, M[('ac', 'ln', st, sub - 1)])
                                t.wait_ge(dv, M[('dv', 'ap', st, sub - 1)])
                        else:
                            if sub >= 1:
                                t.wait_ge(dv, M[('dv', 'acc', st, e, sub - 1)])
                            elif e >= 1:
                                t.wait_ge(dv, M[('dv', 'acc', st, e - 1, NSUB - 1)])
                            elif st >= 1:
                                t.wait_ge(ac, M[('ac', 'ln', st - 1, NSUB - 1)])
                                t.wait_ge(dv, M[('dv', 'ap', st - 1, NSUB - 1)])
                        for n in range(2):
                            for hk in range(NHCm):
                                t.matmul(ps_d[:, n * 512:(n + 1) * 512],
                                         hT[:, hk * TST + sub * 128:
                                            hk * TST + (sub + 1) * 128],
                                         wd_sb[:, hk * D + n * 512:
                                               hk * D + (n + 1) * 512],
                                         start=(hk == 0),
                                         stop=(l0 and e < 7 and hk == NHCm - 1))
                            if not l0:
                                t.matmul(ps_d[:, n * 512:(n + 1) * 512],
                                         ones1[:],
                                         b2_sb[e % 2][0:1, n * 512:(n + 1) * 512],
                                         start=False, stop=True)
                            elif e == 7:
                                t.matmul(ps_d[:, n * 512:(n + 1) * 512],
                                         id_sb[:],
                                         acc[:, sub * D + n * 512:
                                             sub * D + (n + 1) * 512],
                                         start=False, stop=True)
                        t.drain().then_inc(pe, 1)
                if not l0:
                    for sub in range(NSUB):
                        if sub == 0:
                            t.wait_ge(dv, M[('dv', 'acc', st, 7, NSUB - 1)])
                        else:
                            t.wait_ge(ac, M[('ac', 'ln', st, sub - 1)])
                            t.wait_ge(dv, M[('dv', 'ap', st, sub - 1)])
                        for n in range(2):
                            t.matmul(ps_d[:, n * 512:(n + 1) * 512], id_sb[:],
                                     acc[:, sub * D + n * 512:
                                         sub * D + (n + 1) * 512],
                                     start=True, stop=True)
                        t.drain().then_inc(pe, 1)

        @block.scalar
        def _(s):
            s.wait_ge(wI, 32 if l0 else 48)
            for st in range(NST):
                for ch in range(8):
                    gl = st * 8 + ch
                    s.wait_ge(pe, M[('pe', 'r', st, ch)])
                    if gl >= 2:
                        s.wait_ge(rB, M[('rB', 'rb', (gl - 2) // 8, (gl - 2) % 8)])
                    s.activation(out=rstat[ch % 2][:, 0:8], in_=ps_r[0:128, 0:8],
                                 func=AF.Exp,
                                 accum_out=rstat[ch % 2][:, 8:9]).then_inc(ac, 1)
                if l0:
                    for e in range(NE):
                        for hc in range(NHCm):
                            lin = lin_of(st, e, hc)
                            s.wait_ge(pe, M[('pe', 'ug', st, e, hc)])
                            if lin >= 2:
                                s.wait_ge(dv, M[('dv', 'h') + unlin(lin - 2)])
                            s.activation(out=g_sb[lin % 2][:], in_=psB[:],
                                         func=AF.Silu).then_inc(ac, 1)
                for sub in range(NSUB):
                    gs = st * NSUB + sub
                    s.wait_ge(pe, M[('pe', 'dn', st, 7, sub)] if l0
                              else M[('pe', 'li', st, sub)])
                    if gs >= 2:
                        s.wait_ge(sB, M[('sB', 'sb', (gs - 2) // NSUB,
                                         (gs - 2) % NSUB)])
                    emit_ln_stats(s, ps_d[:], sq_f[:],
                                  stats[sub % 2]).then_inc(ac, 1)

        @block.vector
        def _(v):
            if not l0:
                v.memset(ones1[:], 1.0).then_inc(dv, 1)
            for st in range(NST):
                v.wait_ge(rB, M[('rB', 'rb', st, 7)])
                emit_coef_batch(v, l0, rst2, mc32, mask64, num64,
                                coef).then_inc(dv, 1)
                for e in range(NE):
                    ge = st * NE + e
                    for hc in range(NHCm):
                        lin = lin_of(st, e, hc)
                        v.wait_ge(pe, M[('pe', 'ug', st, e, hc)])
                        if l0:
                            v.wait_ge(ac, M[('ac', 'g', st, e, hc)])
                        elif hc == 0:
                            v.wait_ge(wD, M[('wD', 'b', st, e)])
                        if hc == 0 and ge >= 1:
                            pge = ge - 1
                            v.wait_ge(pe, M[('pe', 'dn', pge // NE, pge % NE,
                                             NSUB - 1)])
                        if l0:
                            v.tensor_mul(hT[:, hc * TST:(hc + 1) * TST],
                                         psA[:], g_sb[lin % 2][:]
                                         ).then_inc(dv, 1)
                        else:
                            v.tensor_scalar(out=hT[:, hc * TST:(hc + 1) * TST],
                                            in0=psA[:],
                                            scalar1=b1_sb[e % 2][:, hc:hc + 1],
                                            scalar2=0.0,
                                            op0=ALU.add, op1=ALU.max
                                            ).then_inc(dv, 1)
                    if (not l0) or e < 7:
                        for sub in range(NSUB):
                            v.wait_ge(pe, M[('pe', 'dn', st, e, sub)])
                            cf_ap = (coef[:, sub * 8 + e:sub * 8 + e + 1]
                                     if dbg_fixed_coef is None
                                     else float(dbg_fixed_coef[e]))
                            v.scalar_tensor_tensor(
                                out=acc[:, sub * D:(sub + 1) * D], in0=ps_d[:],
                                scalar=cf_ap,
                                in1=acc[:, sub * D:(sub + 1) * D],
                                op0=ALU.mult, op1=ALU.add
                                ).then_inc(dv, 1)
                for sub in range(NSUB):
                    gs = st * NSUB + sub
                    v.wait_ge(sB, M[('sB', 'sb', st, sub)])
                    if gs >= 2:
                        v.wait_ge(stO, M[('stO', 'o', (gs - 2) // NSUB,
                                          (gs - 2) % NSUB)])
                    v.tensor_scalar(out=x2f[sub % 2][:], in0=ps_d[:],
                                    scalar1=stats2[sub % 2][:, 1:2],
                                    scalar2=stats2[sub % 2][:, 3:4],
                                    op0=ALU.add, op1=ALU.mult
                                    ).then_inc(dv, 1)
    return nc


def build_l0ffn_old_unused():
    nc = new_nc()
    x1 = nc.dram_tensor("x1", [ROWS, D], F32, kind="ExternalInput").ap()
    x1b = nc.dram_tensor("x1b", [ROWS, D], BF16, kind="ExternalInput").ap()
    wr = nc.dram_tensor("wr", [D, 8], F32, kind="ExternalInput").ap()
    wu = nc.dram_tensor("wu", [NE, NHC, 128, 1024], F32, kind="ExternalInput").ap()
    wg = nc.dram_tensor("wg", [NE, NHC, 128, 1024], F32, kind="ExternalInput").ap()
    wd = nc.dram_tensor("wd", [NE, NHC, 128, 1024], F32, kind="ExternalInput").ap()
    ident = nc.dram_tensor("ident", [128, 128], BF16, kind="ExternalInput").ap()
    x2 = nc.dram_tensor("x2", [ROWS, D], F32, kind="ExternalOutput").ap()
    x2b = nc.dram_tensor("x2b", [ROWS, D], BF16, kind="ExternalOutput").ap()

    PE_ST = 8 + NE * (NHC + NSUB)

    def pe_router(st, ch): return st * PE_ST + ch + 1
    def pe_hc(st, e, hc): return st * PE_ST + 8 + e * (NHC + NSUB) + hc + 1
    def pe_sub(st, e, sub):
        return st * PE_ST + 8 + e * (NHC + NSUB) + NHC + sub + 1
    AC_ST = 8 + NE * NHC + NSUB

    def ac_router(st, ch): return st * AC_ST + ch + 1
    def ac_silu(st, e, hc): return st * AC_ST + 8 + e * NHC + hc + 1
    def ac_ln(st, sub): return st * AC_ST + 8 + NE * NHC + sub + 1
    DV_ST = 8 + NE * NHC + 7 * NSUB + NSUB

    def dv_e_base(st, e):
        return st * DV_ST + 8 + sum((NHC + (NSUB if e_ < 7 else 0))
                                    for e_ in range(e))
    def dv_coef(st, ch): return st * DV_ST + ch + 1
    def dv_mult(st, e, hc): return dv_e_base(st, e) + hc + 1
    def dv_multL(st, lin): return dv_mult(st, lin // NHC, lin % NHC) if lin >= 0 else (
        dv_mult(st - 1, 7, NHC + lin) if st > 0 else 0)
    def dv_stt(st, e, sub): return dv_e_base(st, e) + NHC + sub + 1
    def dv_x2(st, sub): return dv_e_base(st, 7) + NHC + sub + 1

    def wU_done(st, e, hc): return 32 * ((st * NE + e) * NHC + hc + 1)
    def wD_done(st, e, hc): return 16 * ((st * NE + e) * NHC + hc + 1)
    def rB_done(st, ch): return 16 * (st * 8 + ch + 1)
    def sB_done(st, sub): return 16 * (st * NSUB + sub + 1)
    def stO_done(st, sub): return 32 * (st * NSUB + sub + 1)

    from contextlib import ExitStack
    with ExitStack() as ctx:
        e_ = ctx.enter_context
        cd, cs, load_consts = decl_consts(nc, e_)
        id_sb = e_(nc.sbuf_tensor("id_sb", [128, 128], BF16))
        wr_sb = e_(nc.sbuf_tensor("wr_sb", [128, 8 * 8], BF16))
        x1T = e_(nc.sbuf_tensor("x1T", [128, 8 * TST], BF16))
        hT = e_(nc.sbuf_tensor("hT", [128, NHC * TST], BF16))
        wd_sb = e_(nc.sbuf_tensor("wd_sb", [128, NHC * D], BF16))
        wu_b = [e_(nc.sbuf_tensor(f"wu_b{j}", [128, D], BF16)) for j in range(2)]
        wg_b = [e_(nc.sbuf_tensor(f"wg_b{j}", [128, D], BF16)) for j in range(2)]
        acc = e_(nc.sbuf_tensor("acc", [128, NSUB * D], BF16))
        g_sb = [e_(nc.sbuf_tensor(f"g_sb{j}", [128, D], F32)) for j in range(2)]
        x2f = [e_(nc.sbuf_tensor(f"x2f{j}", [128, D], F32)) for j in range(2)]
        sq_f = e_(nc.sbuf_tensor("sq_f", [128, D], BF16))
        coef = e_(nc.sbuf_tensor("coef", [128, 64], F32))
        rstat = [e_(nc.sbuf_tensor(f"rstat{j}", [128, 9], F32)) for j in range(2)]
        rstat2 = [e_(nc.sbuf_tensor(f"rstat2{j}", [128, 9], F32)) for j in range(2)]
        stats = [e_(nc.sbuf_tensor(f"stats{j}", [128, 4], F32)) for j in range(2)]
        stats2 = [e_(nc.sbuf_tensor(f"stats2{j}", [128, 4], F32)) for j in range(2)]
        mcol = e_(nc.sbuf_tensor("mcol", [128, 2], F32))
        rcpc = e_(nc.sbuf_tensor("rcpc", [128, 1], F32))
        junk64 = e_(nc.sbuf_tensor("junk64", [128, 64], F32))
        mask8 = e_(nc.sbuf_tensor("mask8", [128, 8], F32))
        spac = e_(nc.sbuf_tensor("spac", [1, len(CONST_VALS)], F32))
        psA = [e_(nc.psum_tensor(f"psA{j}", [128, D], F32)) for j in range(2)]
        psB = e_(nc.psum_tensor("psB", [128, D], F32))
        ps_d = e_(nc.psum_tensor("ps_d", [128, D], F32))

        wI = e_(nc.semaphore("wI"))
        wS = e_(nc.semaphore("wS"))
        tD = e_(nc.semaphore("tD"))
        aL = e_(nc.semaphore("aL"))
        wU = e_(nc.semaphore("wU"))
        wD = e_(nc.semaphore("wD"))
        rB = e_(nc.semaphore("rB"))
        sB = e_(nc.semaphore("sB"))
        pe = e_(nc.semaphore("pe"))
        ac = e_(nc.semaphore("ac"))
        dv = e_(nc.semaphore("dv"))
        stO = e_(nc.semaphore("stO"))
        block = e_(nc.Block())

        @block.sync
        def _(sync):
            sync.dma_start(out=id_sb[:], in_=ident[:]).then_inc(wS, 16)
            for st in range(NST):
                if st >= 1:
                    sync.wait_ge(pe, pe_hc(st - 1, 7, NHC - 1))
                for c in range(8):
                    sync.dma_start(
                        out=x1T[:, c * TST:(c + 1) * TST],
                        in_=x1b[st * TST:(st + 1) * TST, c * 128:(c + 1) * 128],
                        transpose=True).then_inc(tD, 16)

        @block.gpsimd
        def _(g):
            g.dma_start(out=wr_sb.ap().rearrange("p (c n) -> p c n", c=8),
                        in_=wr.rearrange("(c p) n -> p c n", p=128)).then_inc(wI, 16)
            load_consts(g, wI)
            for st in range(NST):
                for sub in range(NSUB):
                    if st >= 1:
                        g.wait_ge(pe, pe_sub(st - 1, 7, sub))
                    g.dma_start(out=acc[:, sub * D:(sub + 1) * D],
                                in_=x1b[st * TST + sub * 128:
                                        st * TST + (sub + 1) * 128, :]
                                ).then_inc(aL, 16)
                for ch in range(8):
                    g.wait_ge(ac, ac_router(st, ch))
                    gl = st * 8 + ch
                    if gl >= 2:
                        g.wait_ge(dv, dv_coef((gl - 2) // 8, (gl - 2) % 8))
                    g.dma_start(out=rstat2[ch % 2][:],
                                in_=rstat[ch % 2][:]).then_inc(rB, 16)
                for e in range(NE):
                    for hc in range(NHC):
                        lin = e * NHC + hc
                        if st > 0 or lin >= 2:
                            g.wait_ge(pe, pe_hc(st, (lin - 2) // NHC, (lin - 2) % NHC)
                                      if lin >= 2 else pe_hc(st - 1, 7, NHC + lin - 2))
                        g.dma_start(out=wu_b[hc % 2][:],
                                    in_=wu[e, hc]).then_inc(wU, 16)
                        g.dma_start(out=wg_b[hc % 2][:],
                                    in_=wg[e, hc]).then_inc(wU, 16)
                        if hc == 0 and (st > 0 or e > 0):
                            g.wait_ge(pe, pe_sub(st, e - 1, NSUB - 1) if e > 0
                                      else pe_sub(st - 1, 7, NSUB - 1))
                        g.dma_start(out=wd_sb[:, hc * D:(hc + 1) * D],
                                    in_=wd[e, hc]).then_inc(wD, 16)
                for sub in range(NSUB):
                    g.wait_ge(ac, ac_ln(st, sub))
                    g.dma_start(out=stats2[sub % 2][:],
                                in_=stats[sub % 2][:]).then_inc(sB, 16)
                    g.wait_ge(dv, dv_x2(st, sub))
                    r0 = st * TST + sub * 128
                    g.dma_start(out=x2[r0:r0 + 128, :],
                                in_=x2f[sub % 2][:]).then_inc(stO, 16)
                    g.dma_start(out=x2b[r0:r0 + 128, :],
                                in_=x2f[sub % 2][:]).then_inc(stO, 16)
            g.wait_ge(stO, 32 * NST * NSUB)

        @block.tensor
        def _(t):
            t.wait_ge(wI, 32)
            t.wait_ge(wS, 16)
            for st in range(NST):
                t.wait_ge(tD, 128 * (st + 1))
                t.wait_ge(aL, 128 * (st + 1))
                for ch in range(8):
                    if st > 0 and ch == 0:
                        t.wait_ge(dv, dv_x2(st - 1, NSUB - 1))
                    if ch >= 1:
                        t.wait_ge(ac, ac_router(st, ch - 1))
                    for c in range(8):
                        t.matmul(ps_d[0:128, 0:8],
                                 x1T[:, c * TST + ch * 128:c * TST + (ch + 1) * 128],
                                 wr_sb[:, c * 8:(c + 1) * 8],
                                 start=(c == 0), stop=(c == 7))
                    t.drain().then_inc(pe, 1)
                for e in range(NE):
                    for hc in range(NHC):
                        lin = e * NHC + hc
                        t.wait_ge(wU, wU_done(st, e, hc))
                        prev2 = st * (NE * NHC) * 0 + lin - 2
                        if st > 0 or lin >= 2:
                            t.wait_ge(dv, dv_multL(st, lin - 2))
                        for tb in range(2):
                            for c in range(8):
                                t.matmul(psA[hc % 2][:, tb * 512:(tb + 1) * 512],
                                         wu_b[hc % 2][:, c * 128:(c + 1) * 128],
                                         x1T[:, c * TST + tb * 512:c * TST + (tb + 1) * 512],
                                         start=(c == 0), stop=(c == 7))
                        if st > 0 or lin >= 1:
                            t.wait_ge(ac, ac_silu(st, (lin - 1) // NHC, (lin - 1) % NHC)
                                      if lin >= 1 else ac_silu(st - 1, 7, NHC - 1))
                        for tb in range(2):
                            for c in range(8):
                                t.matmul(psB[:, tb * 512:(tb + 1) * 512],
                                         wg_b[hc % 2][:, c * 128:(c + 1) * 128],
                                         x1T[:, c * TST + tb * 512:c * TST + (tb + 1) * 512],
                                         start=(c == 0), stop=(c == 7))
                        t.drain().then_inc(pe, 1)
                    t.wait_ge(wD, wD_done(st, e, NHC - 1))
                    t.wait_ge(dv, dv_mult(st, e, NHC - 1))
                    for sub in range(NSUB):
                        if e == 7:
                            t.wait_ge(dv, dv_stt(st, 6, sub))
                        if sub > 0:
                            t.wait_ge(dv, (dv_stt(st, e, sub - 1) if e < 7
                                           else dv_x2(st, sub - 1)))
                        elif e > 0:
                            t.wait_ge(dv, dv_stt(st, e - 1, NSUB - 1) if e - 1 < 7
                                      else 0)
                        if e == 0 and sub == 0:
                            t.wait_ge(ac, ac_router(st, 7))
                        for n in range(2):
                            for hk in range(NHC):
                                t.matmul(ps_d[:, n * 512:(n + 1) * 512],
                                         hT[:, hk * TST + sub * 128:
                                            hk * TST + (sub + 1) * 128],
                                         wd_sb[:, hk * D + n * 512:hk * D + (n + 1) * 512],
                                         start=(hk == 0),
                                         stop=(e < 7 and hk == NHC - 1))
                            if e == 7:
                                t.matmul(ps_d[:, n * 512:(n + 1) * 512],
                                         id_sb[:],
                                         acc[:, sub * D + n * 512:sub * D + (n + 1) * 512],
                                         start=False, stop=True)
                        t.drain().then_inc(pe, 1)

        @block.scalar
        def _(s):
            s.wait_ge(wI, 32)
            for st in range(NST):
                for ch in range(8):
                    s.wait_ge(pe, pe_router(st, ch))
                    if st * 8 + ch >= 2:
                        pch = st * 8 + ch - 2
                        s.wait_ge(rB, rB_done(pch // 8, pch % 8))
                    s.activation(out=sq_f[0:1, 0:256], in_=sq_f[0:1, 0:256],
                                 func=AF.Copy)
                    s.activation(out=rstat[ch % 2][:, 0:8], in_=ps_d[0:128, 0:8],
                                 func=AF.Exp,
                                 accum_out=rstat[ch % 2][:, 8:9]).then_inc(ac, 1)
                for e in range(NE):
                    for hc in range(NHC):
                        lin = e * NHC + hc
                        s.wait_ge(pe, pe_hc(st, e, hc))
                        if st > 0 or lin >= 2:
                            s.wait_ge(dv, dv_multL(st, lin - 2))
                        s.activation(out=g_sb[hc % 2][:], in_=psB[:],
                                     func=AF.Silu).then_inc(ac, 1)
                for sub in range(NSUB):
                    s.wait_ge(pe, pe_sub(st, 7, sub))
                    if st * NSUB + sub >= 2:
                        ps_ = st * NSUB + sub - 2
                        s.wait_ge(sB, sB_done(ps_ // NSUB, ps_ % NSUB))
                    emit_ln_stats(s, ps_d[:], sq_f[:],
                                  stats[sub % 2]).then_inc(ac, 1)

        @block.vector
        def _(v):
            for st in range(NST):
                for ch in range(8):
                    v.wait_ge(rB, rB_done(st, ch))
                    v.reduce_max(mcol[:, 0:1], rstat2[ch % 2][:, 0:8], axis=AX.X)
                    v.reciprocal(rcpc[:], rstat2[ch % 2][:, 8:9])
                    v.tensor_copy(spac[:], cs[0:1, :])
                    v.tensor_tensor(out=mask8[:], in0=rstat2[ch % 2][:, 0:8],
                                    in1=mcol[:, 0:1].broadcast_to([128, 8]),
                                    op=ALU.is_equal)
                    v.tensor_copy(junk64[:], coef[:, 0:64])
                    v.tensor_copy(junk64[:], coef[:, 0:64])
                    v.tensor_mul(mcol[:, 1:2], mcol[:, 0:1], rcpc[:])
                    v.tensor_scalar(out=mcol[:, 1:2], in0=mcol[:, 1:2],
                                    scalar1=2.0, scalar2=None, op0=ALU.mult)
                    v.tensor_copy(junk64[:], coef[:, 0:64])
                    v.tensor_copy(junk64[:], coef[:, 0:64])
                    v.tensor_tensor(out=coef[:, ch * 8:(ch + 1) * 8], in0=mask8[:],
                                    in1=mcol[:, 1:2].broadcast_to([128, 8]),
                                    op=ALU.mult).then_inc(dv, 1)
                for e in range(NE):
                    for hc in range(NHC):
                        lin = e * NHC + hc
                        v.wait_ge(ac, ac_silu(st, e, hc))
                        if hc == 0 and (st > 0 or e > 0):
                            v.wait_ge(pe, pe_sub(st, e - 1, NSUB - 1) if e > 0
                                      else pe_sub(st - 1, 7, NSUB - 1))
                        v.tensor_mul(hT.ap().rearrange("p (h n) -> p h n", h=NHC)[:, hc, :],
                                     psA[hc % 2][:], g_sb[hc % 2][:]).then_inc(dv, 1)
                    if e < 7:
                        for sub in range(NSUB):
                            v.wait_ge(pe, pe_sub(st, e, sub))
                            v.scalar_tensor_tensor(
                                out=acc[:, sub * D:(sub + 1) * D], in0=ps_d[:],
                                scalar=coef[:, (sub * 8 + e):(sub * 8 + e) + 1],
                                in1=acc[:, sub * D:(sub + 1) * D],
                                op0=ALU.mult, op1=ALU.add).then_inc(dv, 1)
                    else:
                        for sub in range(NSUB):
                            v.wait_ge(sB, sB_done(st, sub))
                            if st * NSUB + sub >= 2:
                                pv = st * NSUB + sub - 2
                                v.wait_ge(stO, stO_done(pv // NSUB, pv % NSUB))
                            v.tensor_scalar(out=x2f[sub % 2][:], in0=ps_d[:],
                                            scalar1=stats2[sub % 2][:, 1:2],
                                            scalar2=stats2[sub % 2][:, 3:4],
                                            op0=ALU.add,
                                            op1=ALU.mult).then_inc(dv, 1)
    return nc



SEQ = 8192
NCHU = 16
SC = 512


def build_l0attn():
    nc = new_nc()
    x = nc.dram_tensor("x", [SEQ, D], F32, kind="ExternalInput").ap()
    wq = nc.dram_tensor("wq", [D, 512], F32, kind="ExternalInput").ap()
    wk = nc.dram_tensor("wk", [D, 512], F32, kind="ExternalInput").ap()
    wv = nc.dram_tensor("wv", [D, 512], F32, kind="ExternalInput").ap()
    proj = nc.dram_tensor("proj", [64, 256], F32, kind="ExternalInput").ap()
    ident = nc.dram_tensor("ident", [128, 128], BF16, kind="ExternalInput").ap()
    attn = nc.dram_tensor("attn", [SEQ, 512], BF16, kind="ExternalOutput").ap()
    xbf = nc.dram_tensor("xbf", [SEQ, D], BF16, kind="Internal").ap()
    kvdram = nc.dram_tensor("kvdram", [80, 2048], BF16, kind="Internal").ap()
    kv_dbg = nc.dram_tensor("kv_dbg", [65, 2048], F32, kind="ExternalOutput").ap()
    kvf_dbg = nc.dram_tensor("kvf_dbg", [128, 16 * 80], BF16, kind="ExternalOutput").ap()

    def pe_kt(c, g): return c * 40 + g + 1
    def pe_v(c, s): return c * 40 + 4 + s + 1
    def pe_kp(c, i): return c * 40 + 8 + i + 1
    PE1 = 16 * 40 + 1

    def pe_qt(c2, g): return PE1 + c2 * 44 + g + 1
    def pe_qp(c2, h): return PE1 + c2 * 44 + 4 + h + 1
    def pe_at(c2, i): return PE1 + c2 * 44 + 12 + i + 1

    def dv_kt(c, g): return c * 40 + g + 1
    def dv_v(c, s): return c * 40 + 4 + s + 1
    def dv_kp(c, i): return c * 40 + 8 + i + 1
    DV1 = 16 * 40 + 5

    def dv_qt(c2, g): return DV1 + c2 * 44 + g + 1
    def dv_qp(c2, h): return DV1 + c2 * 44 + 4 + h + 1
    def dv_at(c2, i): return DV1 + c2 * 44 + 12 + i + 1

    from contextlib import ExitStack
    with ExitStack() as ctx:
        e_ = ctx.enter_context
        cd, cs, load_consts = decl_consts(nc, e_)
        id_sb = e_(nc.sbuf_tensor("id_sb", [128, 128], BF16))
        wq_sb = e_(nc.sbuf_tensor("wq_sb", [128, 8 * 512], BF16))
        wk_sb = e_(nc.sbuf_tensor("wk_sb", [128, 8 * 512], BF16))
        wv_sb = e_(nc.sbuf_tensor("wv_sb", [128, 8 * 512], BF16))
        pj_sb = e_(nc.sbuf_tensor("pj_sb", [64, 256], BF16))
        xT = [e_(nc.sbuf_tensor(f"xT{j}", [128, 8 * SC], BF16)) for j in range(2)]
        kT_sb = e_(nc.sbuf_tensor("kT_sb", [64, 8 * SC], BF16))
        v_sb = e_(nc.sbuf_tensor("v_sb", [128, 4 * 8 * 65], BF16))
        kp_sb = e_(nc.sbuf_tensor("kp_sb", [128, 32 * 256], BF16))
        kv_f = e_(nc.sbuf_tensor("kv_f", [65, 4 * 512], F32))
        kv_b = e_(nc.sbuf_tensor("kv_b", [80, 2048], BF16))
        kvf_sb = e_(nc.sbuf_tensor("kvf_sb", [128, 16 * 80], BF16))
        qp_sb = e_(nc.sbuf_tensor("qp_sb", [128, 8 * 2 * SC], BF16))
        at_sb = [e_(nc.sbuf_tensor(f"at_sb{j}", [128, 512], BF16)) for j in range(2)]
        rcpc = e_(nc.sbuf_tensor("rcpc", [128, 1], F32))
        junk64 = e_(nc.sbuf_tensor("junk64", [128, 64], F32))
        spac = e_(nc.sbuf_tensor("spac", [1, len(CONST_VALS)], F32))
        nst = [e_(nc.sbuf_tensor(f"nst{j}", [128, 8], F32)) for j in range(3)]
        nst2 = [e_(nc.sbuf_tensor(f"nst2{j}", [128, 8], F32)) for j in range(3)]
        ast = [e_(nc.sbuf_tensor(f"ast{j}", [128, 512], F32)) for j in range(3)]
        rcp8 = [e_(nc.sbuf_tensor(f"rcp8{j}", [128, 8], F32)) for j in range(3)]
        rcp8b = [e_(nc.sbuf_tensor(f"rcp8b{j}", [128, 8], F32)) for j in range(3)]
        pkv = [e_(nc.psum_tensor(f"pkv{g}", [128, 512], F32)) for g in range(4)]
        pkq = e_(nc.psum_tensor("pkq", [128, 1024], F32))
        px = e_(nc.psum_tensor("px", [128, 512], F32))
        pv = e_(nc.psum_tensor("pv", [128, 512], F32))

        wI = e_(nc.semaphore("wI"))
        wS = e_(nc.semaphore("wS"))
        tD = e_(nc.semaphore("tD"))
        kvD = e_(nc.semaphore("kvD"))
        kvS = e_(nc.semaphore("kvS"))
        nB = e_(nc.semaphore("nB"))
        rB2 = e_(nc.semaphore("rB2"))
        ac2 = e_(nc.semaphore("ac2"))
        dvM = e_(nc.semaphore("dvM"))
        pe = e_(nc.semaphore("pe"))
        dv = e_(nc.semaphore("dv"))
        stO = e_(nc.semaphore("stO"))
        block = e_(nc.Block())

        @block.gpsimd
        def _(g):
            g.dma_start(out=xbf[:], in_=x[:]).then_inc(wI, 16)
            for (wsb, wdr) in ((wq_sb, wq), (wk_sb, wk), (wv_sb, wv)):
                g.dma_start(out=wsb.ap().rearrange("p (c n) -> p c n", c=8),
                            in_=wdr.rearrange("(c p) n -> p c n", p=128)
                            ).then_inc(wI, 16)
            g.dma_start(out=pj_sb[:], in_=proj[:]).then_inc(wI, 16)
            load_consts(g, wI)
            g.wait_ge(dv, DV1)
            g.dma_start(out=kvdram[:], in_=kv_b[:]).then_inc(kvS, 16)
            g.dma_start(out=kv_dbg[:], in_=kv_f[:]).then_inc(kvS, 16)
            g.wait_ge(kvD, 16 * 16)
            g.dma_start(out=kvf_dbg[:], in_=kvf_sb[:]).then_inc(kvS, 16)
            for gx in range(NCHU * 4):
                p3 = gx % 3
                c2, s = gx // 4, gx % 4
                g.wait_ge(dv, 1 + dv_at(c2, s * 8 + 7))
                if gx >= 3:
                    g.wait_ge(ac2, gx - 2)
                g.dma_start(out=nst2[p3][:], in_=nst[p3][:]).then_inc(nB, 16)
                g.wait_ge(ac2, gx + 1)
                if gx >= 3:
                    g.wait_ge(dvM, gx - 2)
                g.dma_start(out=rcp8b[p3][:], in_=rcp8[p3][:]).then_inc(rB2, 16)
                g.wait_ge(dvM, gx + 1)
                r0 = c2 * SC + s * 128
                g.dma_start(out=attn[r0:r0 + 128, :],
                            in_=at_sb[gx % 2][:]).then_inc(stO, 16)
            g.wait_ge(stO, 16 * 64)

        @block.sync
        def _(sync):
            sync.dma_start(out=id_sb[:], in_=ident[:]).then_inc(wS, 16)
            sync.wait_ge(wI, 16)
            for c in range(NCHU):
                if c >= 2:
                    sync.wait_ge(pe, pe_kp(c - 2, 31))
                for cc in range(8):
                    sync.dma_start(out=xT[c % 2][:, cc * SC:(cc + 1) * SC],
                                   in_=xbf[c * SC:(c + 1) * SC,
                                           cc * 128:(cc + 1) * 128],
                                   transpose=True).then_inc(tD, 16)
            sync.wait_ge(kvS, 16)
            for i in range(16):
                sync.dma_start(out=kvf_sb[:, i * 80:(i + 1) * 80],
                               in_=kvdram[:, i * 128:(i + 1) * 128],
                               transpose=True).then_inc(kvD, 16)
            for c2 in range(NCHU):
                if c2 >= 2:
                    sync.wait_ge(pe, pe_at(c2 - 2, 31))
                for cc in range(8):
                    sync.dma_start(out=xT[c2 % 2][:, cc * SC:(cc + 1) * SC],
                                   in_=xbf[c2 * SC:(c2 + 1) * SC,
                                           cc * 128:(cc + 1) * 128],
                                   transpose=True).then_inc(tD, 16)

        @block.tensor
        def _(t):
            t.wait_ge(wI, 96)
            t.wait_ge(wS, 16)
            t.wait_ge(dv, 1)
            for c in range(NCHU):
                t.wait_ge(tD, 128 * (c + 1))
                for g in range(4):
                    if c > 0 or g >= 2:
                        t.wait_ge(dv, 1 + (dv_kt(c, g - 2) if g >= 2
                                           else dv_kt(c - 1, 2 + g)))
                    for cc in range(8):
                        t.matmul(pkq[:, (g % 2) * 512:(g % 2) * 512 + 512],
                                 wk_sb[:, cc * 512 + g * 128:cc * 512 + (g + 1) * 128],
                                 xT[c % 2][:, cc * SC:(cc + 1) * SC],
                                 start=(cc == 0), stop=(cc == 7))
                    t.drain().then_inc(pe, 1)
                for s in range(4):
                    if c > 0 or s >= 1:
                        t.wait_ge(dv, 1 + (dv_v(c, s - 1) if s >= 1
                                           else dv_v(c - 1, 3)))
                    for cc in range(8):
                        t.matmul(pv[:],
                                 xT[c % 2][:, cc * SC + s * 128:cc * SC + (s + 1) * 128],
                                 wv_sb[:, cc * 512:(cc + 1) * 512],
                                 start=(cc == 0), stop=(cc == 7))
                    t.drain().then_inc(pe, 1)
                for i in range(32):
                    s, h = i // 8, i % 8
                    if c > 0 or i >= 2:
                        t.wait_ge(dv, 1 + (dv_kp(c, i - 2) if i >= 2
                                           else dv_kp(c - 1, 30 + i)))
                    if i == 1:
                        t.wait_ge(dv, 1 + dv_v(c, 3))
                    t.wait_ge(dv, 1 + dv_kt(c, h // 2))
                    t.matmul((px if i % 2 == 0 else pv)[:, 0:256],
                             kT_sb[:, h * SC + s * 128:h * SC + (s + 1) * 128],
                             pj_sb[:], start=True, stop=True)
                    t.drain().then_inc(pe, 1)
                for i in range(32):
                    s, h = i // 8, i % 8
                    t.wait_ge(dv, 1 + dv_kp(c, i))
                    ins = t.matmul(pkv[h // 2][0:65, (h % 2) * 256:(h % 2) * 256 + 256],
                                   v_sb[:, (s * 8 + h) * 65:(s * 8 + h) * 65 + 65],
                                   kp_sb[:, i * 256:(i + 1) * 256],
                                   start=(c == 0 and s == 0 and h % 2 == 0),
                                   stop=(c == NCHU - 1 and s == 3 and h % 2 == 1),
                                   skip_group_check=True)
            ins.then_inc(pe, 1)
            for c2 in range(NCHU):
                t.wait_ge(tD, 128 * (16 + c2 + 1))
                for g in range(4):
                    if c2 > 0 or g >= 2:
                        t.wait_ge(dv, 1 + (dv_qt(c2, g - 2) if g >= 2
                                           else dv_qt(c2 - 1, 2 + g)))
                    if c2 == 0 and g == 0:
                        t.wait_ge(kvD, 16 * 16)
                    for cc in range(8):
                        t.matmul(pkq[:, (g % 2) * 512:(g % 2) * 512 + 512],
                                 wq_sb[:, cc * 512 + g * 128:cc * 512 + (g + 1) * 128],
                                 xT[c2 % 2][:, cc * SC:(cc + 1) * SC],
                                 start=(cc == 0), stop=(cc == 7))
                    t.drain().then_inc(pe, 1)
                for h in range(8):
                    if c2 > 0 or h >= 1:
                        t.wait_ge(dv, 1 + (dv_qp(c2, h - 1) if h >= 1
                                           else dv_qp(c2 - 1, 7)))
                    t.wait_ge(dv, 1 + dv_qt(c2, h // 2))
                    for fc in range(2):
                        t.matmul((px if fc == 0 else pv)[:, 0:512],
                                 pj_sb[:, fc * 128:(fc + 1) * 128],
                                 kT_sb[:, h * SC:(h + 1) * SC],
                                 start=True, stop=True)
                    t.drain().then_inc(pe, 1)
                for i in range(32):
                    s, h = i // 8, i % 8
                    t.wait_ge(dv, 1 + dv_qp(c2, h))
                    if c2 > 0 or i >= 4:
                        pi = i - 4
                        t.wait_ge(dv, 1 + (dv_at(c2, pi) if pi >= 0
                                           else dv_at(c2 - 1, 28 + i)))
                    for fc in range(2):
                        t.matmul(pkv[i % 4][0:128, 0:65],
                                 qp_sb[:, (h * 2 + fc) * SC + s * 128:
                                       (h * 2 + fc) * SC + (s + 1) * 128],
                                 kvf_sb[:, (h * 2 + fc) * 80:(h * 2 + fc) * 80 + 65],
                                 start=(fc == 0), stop=(fc == 1))
                    t.drain().then_inc(pe, 1)

        @block.scalar
        def _(sc_):
            sc_.wait_ge(wI, 96)
            for gx in range(NCHU * 4):
                p3 = gx % 3
                sc_.wait_ge(nB, 16 * (gx + 1))
                if gx >= 3:
                    sc_.wait_ge(rB2, 16 * (gx - 2))
                sc_.activation(out=rcp8[p3][:], in_=nst2[p3][:], func=AF.Ln,
                               bias=1e-8)
                sc_.activation(out=rcp8[p3][:], in_=rcp8[p3][:], func=AF.Exp,
                               scale=-1.0).then_inc(ac2, 1)

        @block.vector
        def _(v):
            v.memset(v_sb[:], 0.0)
            v.memset(v_sb.ap().rearrange("p (i n) -> p i n", n=65)[:, :, 0:1],
                     1.0).then_inc(dv, 1)
            for c in range(NCHU):
                for g in range(4):
                    v.wait_ge(pe, pe_kt(c, g))
                    v.tensor_copy(kT_sb[:, (2 * g) * SC:(2 * g) * SC + SC],
                                  pkq[0:64, (g % 2) * 512:(g % 2) * 512 + 512])
                    v.tensor_copy(kT_sb[:, (2 * g + 1) * SC:(2 * g + 1) * SC + SC],
                                  pkq[64:128, (g % 2) * 512:(g % 2) * 512 + 512]
                                  ).then_inc(dv, 1)
                for s in range(4):
                    v.wait_ge(pe, pe_v(c, s))
                    v.tensor_copy(
                        v_sb.ap().rearrange("p (i n) -> p i n", n=65)
                        [:, s * 8:(s + 1) * 8, 1:65],
                        pv.ap().rearrange("p (h d) -> p h d", h=8)
                        ).then_inc(dv, 1)
                for i in range(32):
                    v.wait_ge(pe, pe_kp(c, i))
                    v.tensor_scalar(out=kp_sb[:, i * 256:(i + 1) * 256],
                                    in0=(px if i % 2 == 0 else pv)[:, 0:256],
                                    scalar1=0.0, scalar2=None,
                                    op0=ALU.max).then_inc(dv, 1)
            v.wait_ge(pe, PE1)
            v.memset(kv_b[64:80, :], 0.0)
            for g in range(4):
                v.tensor_copy(kv_f[:, g * 512:(g + 1) * 512], pkv[g][0:65, :])
            v.tensor_copy(kv_b[0:65, :], kv_f[:]).then_inc(dv, 5)
            for c2 in range(NCHU):
                for g in range(4):
                    v.wait_ge(pe, pe_qt(c2, g))
                    v.tensor_copy(kT_sb[:, (2 * g) * SC:(2 * g) * SC + SC],
                                  pkq[0:64, (g % 2) * 512:(g % 2) * 512 + 512])
                    v.tensor_copy(kT_sb[:, (2 * g + 1) * SC:(2 * g + 1) * SC + SC],
                                  pkq[64:128, (g % 2) * 512:(g % 2) * 512 + 512]
                                  ).then_inc(dv, 1)
                for h in range(8):
                    v.wait_ge(pe, pe_qp(c2, h))
                    if c2 >= 1:
                        v.wait_ge(pe, pe_at(c2 - 1, 24 + h))
                    v.tensor_scalar(out=qp_sb[:, (2 * h) * SC:(2 * h + 1) * SC],
                                    in0=px[:],
                                    scalar1=0.0, scalar2=None, op0=ALU.max)
                    v.tensor_scalar(out=qp_sb[:, (2 * h + 1) * SC:(2 * h + 2) * SC],
                                    in0=pv[:],
                                    scalar1=0.0, scalar2=None,
                                    op0=ALU.max).then_inc(dv, 1)
                def emit_mults(gx):
                    p3 = gx % 3
                    v.wait_ge(rB2, 16 * (gx + 1))
                    if gx >= 2:
                        v.wait_ge(stO, 16 * (gx - 1))
                    for h_ in range(8):
                        ins = v.tensor_tensor(
                            out=at_sb[gx % 2][:, h_ * 64:(h_ + 1) * 64],
                            in0=ast[p3][:, h_ * 64:(h_ + 1) * 64],
                            in1=rcp8b[p3][:, h_:h_ + 1].broadcast_to([128, 64]),
                            op=ALU.mult)
                    ins.then_inc(dvM, 1)
                for i in range(32):
                    s, h = i // 8, i % 8
                    gx = c2 * 4 + s
                    v.wait_ge(pe, pe_at(c2, i))
                    if h == 0:
                        if gx >= 3:
                            v.wait_ge(nB, 16 * (gx - 2))
                            v.wait_ge(dvM, gx - 2)
                    bank = pkv[i % 4]
                    p3 = gx % 3
                    v.tensor_copy(nst[p3][:, h:h + 1], bank[0:128, 0:1])
                    v.tensor_copy(ast[p3][:, h * 64:(h + 1) * 64],
                                  bank[0:128, 1:65]).then_inc(dv, 1)
                    if i % 8 == 7 and gx >= 2:
                        emit_mults(gx - 2)
            emit_mults(NCHU * 4 - 2)
            emit_mults(NCHU * 4 - 1)
    return nc


def run_l0attn(x_full, Wq, Wk, Wv, proj):
    nc = build_l0attn()
    in_maps = []
    for c in range(NCORES):
        b, hg = c // 2, c % 2
        in_maps.append({
            "x": np.ascontiguousarray(x_full[b]),
            "wq": np.ascontiguousarray(Wq[:, hg * 512:(hg + 1) * 512]),
            "wk": np.ascontiguousarray(Wk[:, hg * 512:(hg + 1) * 512]),
            "wv": np.ascontiguousarray(Wv[:, hg * 512:(hg + 1) * 512]),
            "proj": np.ascontiguousarray(proj),
            "ident": IDENT, "consts": CONSTS_NP,
        })
    res = run_bass_kernel_spmd(nc, in_maps, core_ids=list(range(NCORES)))
    out = np.empty((B, SEQ, D), dtype=BF)
    for c in range(NCORES):
        b, hg = c // 2, c % 2
        out[b, :, hg * 512:(hg + 1) * 512] = res.results[c]["attn"]
    return out



TPAD = 4224


def build_l1pre():
    nc = new_nc()
    x2h = nc.dram_tensor("x2h", [TPAD, D], F32, kind="ExternalInput").ap()
    x2hb = nc.dram_tensor("x2hb", [TPAD, D], BF16, kind="ExternalInput").ap()
    wi = nc.dram_tensor("wi", [32, 128, 1024], F32, kind="ExternalInput").ap()
    ib = nc.dram_tensor("ib", [128, 32], F32, kind="ExternalInput").ap()
    cw = nc.dram_tensor("cw", [128, 16 * 4], F32, kind="ExternalInput").ap()
    cb = nc.dram_tensor("cb", [128, 16], F32, kind="ExternalInput").ap()
    xpj = nc.dram_tensor("xpj", [16, 128, 33], F32, kind="ExternalInput").ap()
    dtw = nc.dram_tensor("dtw", [1, 2048], F32, kind="ExternalInput").ap()
    dtb = nc.dram_tensor("dtb", [128, 16], F32, kind="ExternalInput").ap()
    hmask = nc.dram_tensor("hmask", [128, 1], F32, kind="ExternalInput").ap()
    xhg_o = nc.dram_tensor("xhg_o", [2048, ROWS], F32, kind="ExternalOutput").ap()
    e_o = nc.dram_tensor("e_o", [2048, ROWS], F32, kind="ExternalOutput").ap()
    bc_o = nc.dram_tensor("bc_o", [32, ROWS], F32, kind="ExternalOutput").ap()

    NB = 9
    BLK = [512] * 8 + [128]
    OFF = [sum(BLK[:i]) for i in range(NB)]

    def pe_x(jc): return 2 * jc + 1
    def pe_z(jc): return 2 * jc + 2
    PEJ = 32 + 1

    from contextlib import ExitStack
    with ExitStack() as ctx:
        e_ = ctx.enter_context
        cd, cs, load_consts = decl_consts(nc, e_)
        wi_b = [e_(nc.sbuf_tensor(f"wi_b{j}", [128, 1024], BF16)) for j in range(2)]
        wiz_b = [e_(nc.sbuf_tensor(f"wiz_b{j}", [128, 1024], BF16)) for j in range(2)]
        xpj_sb = e_(nc.sbuf_tensor("xpj_sb", [128, 16 * 33], BF16))
        ib_sb = e_(nc.sbuf_tensor("ib_sb", [128, 32], F32))
        cw_sb = e_(nc.sbuf_tensor("cw_sb", [128, 16 * 4], F32))
        cb_sb = e_(nc.sbuf_tensor("cb_sb", [128, 16], F32))
        dtw_sb = e_(nc.sbuf_tensor("dtw_sb", [1, 2048], BF16))
        dtb_sb = e_(nc.sbuf_tensor("dtb_sb", [128, 16], F32))
        hm_sb = e_(nc.sbuf_tensor("hm_sb", [128, 1], F32))
        xT = e_(nc.sbuf_tensor("xT", [128, 8 * TPAD], BF16))
        xh_c = [e_(nc.sbuf_tensor(f"xh_c{j}", [128, TPAD], F32)) for j in range(2)]
        sz_c = [e_(nc.sbuf_tensor("sz_c0", [128, ROWS], F32))] * 2
        cv_c = e_(nc.sbuf_tensor("cv_c", [128, ROWS], F32))
        xg_c = [e_(nc.sbuf_tensor(f"xg_c{j}", [128, ROWS], F32)) for j in range(2)]
        xgb_c = [e_(nc.sbuf_tensor("xgb_c0", [128, ROWS], BF16))] * 2
        pj_acc = e_(nc.sbuf_tensor("pj_acc", [33, ROWS], F32))

        spacv = e_(nc.sbuf_tensor("spacv", [1, len(CONST_VALS)], F32))
        pA = e_(nc.psum_tensor("pA", [128, 1024], F32))
        pB = e_(nc.psum_tensor("pB", [128, 1024], F32))
        pP = e_(nc.psum_tensor("pP", [128, 1024], F32))

        wI = e_(nc.semaphore("wI"))
        tD = e_(nc.semaphore("tD"))
        pe = e_(nc.semaphore("pe"))
        dv = e_(nc.semaphore("dv"))
        ac = e_(nc.semaphore("ac"))
        stO = e_(nc.semaphore("stO"))
        block = e_(nc.Block())

        cnt = {"pe": 0, "dv": 0, "ac": 0, "stO": 0}
        M = {}

        def mark(sem, key):
            cnt[sem] += 16 if sem == "stO" else 1
            M[(sem, key)] = cnt[sem]
        for jc in range(16):
            for nb in range(NB):
                mark("pe", ("xh", jc, nb))
                mark("dv", ("xhe", jc, nb))
            for nb in range(NB):
                mark("pe", ("z", jc, nb))
                mark("ac", ("sle", jc, nb))
            mark("ac", ("silu", jc))
            mark("dv", ("xg", jc))
            for nb in range(8):
                mark("pe", ("xp", jc, nb))
                mark("dv", ("pje", jc, nb))
            mark("dv", ("pj", jc))
            mark("stO", ("xhg", jc))
        mark("dv", ("dlt",))
        for jc in range(16):
            for nb in range(8):
                mark("pe", ("ee", jc, nb))
                mark("ac", ("eev", jc, nb))
            mark("ac", ("ee", jc))
            mark("stO", ("e", jc))
        mark("stO", ("bc",))

        @block.sync
        def _(sync):
            for c in range(8):
                sync.dma_start(out=xT[:, c * TPAD:(c + 1) * TPAD],
                               in_=x2hb[:, c * 128:(c + 1) * 128],
                               transpose=True).then_inc(tD, 16)

        e_c = [xh_c[0], xh_c[1]]


        @block.gpsimd
        def _(g):
            load_consts(g, wI)
            g.dma_start(out=xpj_sb.ap().rearrange("p (c n) -> p c n", c=16),
                        in_=xpj.rearrange("c p n -> p c n")).then_inc(wI, 16)
            g.dma_start(out=ib_sb[:], in_=ib[:]).then_inc(wI, 16)
            g.dma_start(out=cw_sb[:], in_=cw[:]).then_inc(wI, 16)
            g.dma_start(out=cb_sb[:], in_=cb[:]).then_inc(wI, 16)
            g.dma_start(out=dtw_sb[:], in_=dtw[:]).then_inc(wI, 16)
            g.dma_start(out=dtb_sb[:], in_=dtb[:]).then_inc(wI, 16)
            g.dma_start(out=hm_sb[:], in_=hmask[:]).then_inc(wI, 16)
            for jc in range(16):
                if jc >= 2:
                    g.wait_ge(pe, M[("pe", ("xh", jc - 2, NB - 1))])
                g.dma_start(out=wi_b[jc % 2][:], in_=wi[jc]).then_inc(wI, 16)
                if jc >= 2:
                    g.wait_ge(pe, M[("pe", ("z", jc - 2, NB - 1))])
                g.dma_start(out=wiz_b[jc % 2][:], in_=wi[16 + jc]).then_inc(wI, 16)
                if jc >= 2:
                    g.wait_ge(dv, M[("dv", ("xg", jc - 2))])
                    g.dma_start(out=xhg_o[128 * (jc - 2):128 * (jc - 1), :],
                                in_=xg_c[jc % 2][:]).then_inc(stO, 16)
            for jc in (14, 15):
                g.wait_ge(dv, M[("dv", ("xg", jc))])
                g.dma_start(out=xhg_o[128 * jc:128 * (jc + 1), :],
                            in_=xg_c[jc % 2][:]).then_inc(stO, 16)
            for jc in range(16):
                g.wait_ge(ac, M[("ac", ("ee", jc))])
                g.dma_start(out=e_o[128 * jc:128 * (jc + 1), :],
                            in_=e_c[jc % 2][:, 0:ROWS]).then_inc(stO, 16)
            g.wait_ge(dv, M[("dv", ("dlt",))])
            g.dma_start(out=bc_o[:], in_=pj_acc[1:33, :]).then_inc(stO, 16)
            g.wait_ge(stO, cnt["stO"])

        @block.tensor
        def _(t):
            t.wait_ge(wI, 8 * 16)
            t.wait_ge(tD, 16 * 8)
            for jc in range(16):
                j2 = jc % 2
                t.wait_ge(wI, 16 * (8 + 2 * jc + 1))
                if jc >= 1:
                    t.wait_ge(dv, M[("dv", ("xg", jc - 1))])
                if jc >= 2:
                    t.wait_ge(dv, M[("dv", ("xg", jc - 2))])
                for nb in range(NB):
                    L = jc * NB + nb
                    if L >= 2:
                        t.wait_ge(dv, M[("dv", ("xhe", (L - 2) // NB, (L - 2) % NB))])
                    bk = pA[:, (nb % 2) * 512:(nb % 2) * 512 + BLK[nb]]
                    for c in range(8):
                        t.matmul(bk,
                                 wi_b[j2][:, c * 128:(c + 1) * 128],
                                 xT[:, c * TPAD + OFF[nb]:c * TPAD + OFF[nb] + BLK[nb]],
                                 start=(c == 0), stop=(c == 7))
                    t.drain().then_inc(pe, 1)
                t.wait_ge(wI, 16 * (8 + 2 * jc + 2))
                for nb in range(NB):
                    L = jc * NB + nb
                    if L >= 2:
                        t.wait_ge(ac, M[("ac", ("sle", (L - 2) // NB, (L - 2) % NB))])
                    bk = pB[:, (nb % 2) * 512:(nb % 2) * 512 + BLK[nb]]
                    for c in range(8):
                        t.matmul(bk,
                                 wiz_b[j2][:, c * 128:(c + 1) * 128],
                                 xT[:, c * TPAD + OFF[nb]:c * TPAD + OFF[nb] + BLK[nb]],
                                 start=(c == 0), stop=(c == 7))
                    t.drain().then_inc(pe, 1)
                t.wait_ge(dv, M[("dv", ("xg", jc))])
                for nb in range(8):
                    L = jc * 8 + nb
                    if L >= 2:
                        t.wait_ge(dv, M[("dv", ("pje", (L - 2) // 8, (L - 2) % 8))])
                    t.matmul(pP[0:33, (nb % 2) * 512:(nb % 2) * 512 + 512],
                             xpj_sb[:, jc * 33:(jc + 1) * 33],
                             xgb_c[jc % 2][:, nb * 512:(nb + 1) * 512],
                             start=True, stop=True)
                    t.drain().then_inc(pe, 1)
            t.wait_ge(dv, M[("dv", ("dlt",))])
            for jc in range(16):
                for nb in range(8):
                    L = jc * 8 + nb
                    if L >= 2:
                        pl = L - 2
                        t.wait_ge(ac, M[("ac", ("eev", pl // 8, pl % 8))])
                    t.matmul((pA if L % 2 == 0 else pB)[:, 0:512],
                             dtw_sb[0:1, jc * 128:(jc + 1) * 128],
                             xgb_c[0][0:1, nb * 512:(nb + 1) * 512],
                             start=True, stop=True)
                    t.drain().then_inc(pe, 1)

        @block.scalar
        def _(s):
            s.wait_ge(wI, 8 * 16)
            for jc in range(16):
                if jc >= 1:
                    s.wait_ge(dv, M[("dv", ("xg", jc - 1))])
                for nb in range(NB):
                    s.wait_ge(pe, M[("pe", ("z", jc, nb))])
                    off, blk = OFF[nb], BLK[nb]
                    lo = max(off, 128) - off
                    if lo < blk:
                        s.activation(out=sz_c[jc % 2][:, off + lo - 128:off + blk - 128],
                                     in_=pB[:, (nb % 2) * 512 + lo:(nb % 2) * 512 + blk],
                                     func=AF.Silu, bias=ib_sb[:, 16 + jc:17 + jc])
                    ins = s.activation(out=sz_c[jc % 2][0:1, 0:8],
                                       in_=sz_c[jc % 2][0:1, 0:8], func=AF.Copy)
                    ins.then_inc(ac, 1)
                ins = s.activation(out=sz_c[jc % 2][0:1, 0:8],
                                   in_=sz_c[jc % 2][0:1, 0:8], func=AF.Copy)
                ins.then_inc(ac, 1)
            for jc in range(16):
                if jc >= 2:
                    s.wait_ge(stO, M[("stO", ("e", jc - 2))])
                for nb in range(8):
                    L = jc * 8 + nb
                    s.wait_ge(pe, M[("pe", ("ee", jc, nb))])
                    s.activation(out=e_c[jc % 2][:, nb * 512:(nb + 1) * 512],
                                 in_=(pA if L % 2 == 0 else pB)[:, 0:512],
                                 func=AF.Exp,
                                 bias=dtb_sb[:, jc:jc + 1]).then_inc(ac, 1)
                ins = s.activation(out=e_c[jc % 2][0:1, 0:8],
                                   in_=e_c[jc % 2][0:1, 0:8], func=AF.Copy)
                ins.then_inc(ac, 1)

        @block.vector
        def _(v):
            for jc in range(16):
                j2 = jc % 2
                for nb in range(NB):
                    v.wait_ge(pe, M[("pe", ("xh", jc, nb))])
                    v.tensor_scalar(out=xh_c[j2][:, OFF[nb]:OFF[nb] + BLK[nb]],
                                    in0=pA[:, (nb % 2) * 512:(nb % 2) * 512 + BLK[nb]],
                                    scalar1=ib_sb[:, jc:jc + 1], scalar2=None,
                                    op0=ALU.add).then_inc(dv, 1)
                v.tensor_scalar(out=xh_c[j2][:, 125:128], in0=xh_c[j2][:, 125:128],
                                scalar1=hm_sb[:], scalar2=None, op0=ALU.mult)
                v.tensor_scalar(out=cv_c[:], in0=xh_c[j2][:, 125:125 + ROWS],
                                scalar1=cw_sb[:, jc * 4:jc * 4 + 1], scalar2=None,
                                op0=ALU.mult)
                for kk in range(1, 4):
                    v.scalar_tensor_tensor(out=cv_c[:],
                                           in0=xh_c[j2][:, 125 + kk:125 + kk + ROWS],
                                           scalar=cw_sb[:, jc * 4 + kk:jc * 4 + kk + 1],
                                           in1=cv_c[:], op0=ALU.mult, op1=ALU.add)
                v.tensor_scalar(out=cv_c[:], in0=cv_c[:],
                                scalar1=cb_sb[:, jc:jc + 1], scalar2=None,
                                op0=ALU.add)
                v.wait_ge(ac, M[("ac", ("silu", jc))])
                if jc >= 2:
                    v.wait_ge(stO, M[("stO", ("xhg", jc - 2))])
                v.tensor_mul(xg_c[j2][:], cv_c[:], sz_c[j2][:])
                if jc >= 1:
                    v.wait_ge(pe, M[("pe", ("xp", jc - 1, 7))])
                v.tensor_copy(xgb_c[j2][:], xg_c[j2][:]).then_inc(dv, 1)
                for nb in range(8):
                    v.wait_ge(pe, M[("pe", ("xp", jc, nb))])
                    src = pP[0:33, (nb % 2) * 512:(nb % 2) * 512 + 512]
                    if jc == 0:
                        v.tensor_copy(pj_acc[:, nb * 512:(nb + 1) * 512], src)
                    else:
                        v.tensor_add(pj_acc[:, nb * 512:(nb + 1) * 512],
                                     pj_acc[:, nb * 512:(nb + 1) * 512], src)
                    v.tensor_copy(spacv[:], cs[0:1, :]).then_inc(dv, 1)
                v.tensor_copy(spacv[:], cs[0:1, :]).then_inc(dv, 1)
            v.tensor_copy(xgb_c[0][0:1, 0:ROWS], pj_acc[0:1, :])
            v.tensor_copy(spacv[:], cs[0:1, :]).then_inc(dv, 1)
    return nc


def prep_upgate(w):
    hid = w.shape[1]
    return np.ascontiguousarray(
        w.reshape(8, 128, hid // 128, 128).transpose(2, 1, 0, 3)
        .reshape(hid // 128, 128, 1024))


def run_l1pre(x2_full, x2b_parts, in_w, in_b, conv_w, conv_b, xproj_w,
              dt_w, dt_b):
    nc = build_l1pre()
    wi_np = np.concatenate([prep_upgate(np.ascontiguousarray(in_w.T[:, 0:2048])),
                            prep_upgate(np.ascontiguousarray(in_w.T[:, 2048:4096]))],
                           axis=0).astype(np.float32)
    ib_np = np.ascontiguousarray(in_b.reshape(32, 128).T).astype(np.float32)
    cw_np = np.ascontiguousarray(
        conv_w.reshape(16, 128, 4).transpose(1, 0, 2).reshape(128, 64)).astype(np.float32)
    cb_np = np.ascontiguousarray(conv_b.reshape(16, 128).T).astype(np.float32)
    xpj_np = np.ascontiguousarray(
        xproj_w.T.reshape(16, 128, 33)).astype(np.float32)
    dtw_np = np.ascontiguousarray(dt_w.reshape(1, 2048)).astype(np.float32)
    dtb_np = np.ascontiguousarray(dt_b.reshape(16, 128).T).astype(np.float32)
    zpadf = np.zeros((128, D), np.float32)
    zpadb = np.zeros((128, D), BF)
    in_maps = []
    for c in range(NCORES):
        b, half = c // 2, c % 2
        rows = x2_full[b].reshape(S, D)
        if half == 0:
            x2h = np.concatenate([zpadf, rows[0:ROWS]], axis=0)
            x2hb = np.concatenate([zpadb, x2b_parts[c]], axis=0)
            hm = np.zeros((128, 1), np.float32)
        else:
            x2h = np.concatenate([zpadf[0:125], rows[ROWS - 3:ROWS],
                                  rows[ROWS:]], axis=0)
            x2hb = np.concatenate([zpadb[0:125], x2b_parts[c - 1][-3:],
                                   x2b_parts[c]], axis=0)
            hm = np.ones((128, 1), np.float32)
        in_maps.append({"x2h": np.ascontiguousarray(x2h),
                        "x2hb": np.ascontiguousarray(x2hb),
                        "wi": wi_np, "ib": ib_np, "cw": cw_np, "cb": cb_np,
                        "xpj": xpj_np, "dtw": dtw_np, "dtb": dtb_np,
                        "hmask": hm, "consts": CONSTS_NP})
    res = run_bass_kernel_spmd(nc, in_maps, core_ids=list(range(NCORES)))
    return res



def build_l1scan():
    nc = new_nc()
    xh = nc.dram_tensor("xh", [1024, S], F32, kind="ExternalInput").ap()
    ee = nc.dram_tensor("ee", [1024, S], F32, kind="ExternalInput").ap()
    bc = nc.dram_tensor("bc", [32, S], F32, kind="ExternalInput").ap()
    alog = nc.dram_tensor("alog", [128, 8 * 16], F32, kind="ExternalInput").ap()
    dvec = nc.dram_tensor("dvec", [128, 8], F32, kind="ExternalInput").ap()
    y_o = nc.dram_tensor("y_o", [1024, S], F32, kind="ExternalOutput").ap()

    NT, NS, HF = 8, 16, S // 2

    from contextlib import ExitStack
    with ExitStack() as ctx:
        e_ = ctx.enter_context
        cd, cs, load_consts = decl_consts(nc, e_)
        xh_t = e_(nc.sbuf_tensor("xh_t", [128, S], F32))
        e_t = e_(nc.sbuf_tensor("e_t", [128, S], F32))
        h_t = e_(nc.sbuf_tensor("h_t", [128, S], F32))
        y_t = e_(nc.sbuf_tensor("y_t", [128, S], F32))
        rep_t = e_(nc.sbuf_tensor("rep_t", [128, HF], F32))
        ab_t = [e_(nc.sbuf_tensor("ab_t0", [128, HF], F32))] * 2
        uc = [e_(nc.sbuf_tensor(f"uc{j}", [128, HF], F32)) for j in range(2)]
        an_raw = e_(nc.sbuf_tensor("an_raw", [128, 16], F32))
        an_sb = e_(nc.sbuf_tensor("an_sb", [128, 16], F32))
        an2_sb = e_(nc.sbuf_tensor("an2_sb", [128, 16], F32))
        alog_sb = e_(nc.sbuf_tensor("alog_sb", [128, 8 * 16], F32))
        dv_sb = e_(nc.sbuf_tensor("dv_sb", [128, 8], F32))
        spac = e_(nc.sbuf_tensor("spac", [1, len(CONST_VALS)], F32))

        wI = e_(nc.semaphore("wI"))
        tL = e_(nc.semaphore("tL"))
        aB = e_(nc.semaphore("aB"))
        pl = e_(nc.semaphore("pl"))
        dv = e_(nc.semaphore("dv"))
        ac = e_(nc.semaphore("ac"))
        block = e_(nc.Block())

        cnt = {"dv": 0, "ac": 0, "pl": 0, "tL": 0, "aB": 0}
        M = {}

        def mark(sem, key, dma=False):
            cnt[sem] += 16 if dma else 1
            M[(sem, key)] = cnt[sem]
        for it in range(NT):
            if it >= 1:
                mark("tL", ("yst", it - 1), True)
            mark("tL", ("xh", it), True)
            mark("tL", ("ee", it), True)
            mark("ac", ("expan", it))
            mark("dv", ("neg", it))
            mark("aB", ("aneg", it), True)
            for s_ in range(NS):
                for hf in range(2):
                    mark("tL", ("b", it, s_, hf), True)
                    mark("ac", ("ab", it, s_, hf))
                    mark("dv", ("u", it, s_, hf))
                    mark("dv", ("h", it, s_, hf))
                for hf in range(2):
                    mark("tL", ("c", it, s_, hf), True)
                    mark("pl", ("hc", it, s_, hf))
                    mark("dv", ("y", it, s_, hf))
            mark("dv", ("yd", it))
        mark("tL", ("yst", NT - 1), True)

        @block.sync
        def _(sync):
            for it in range(NT):
                if it >= 1:
                    sync.wait_ge(dv, M[("dv", ("yd", it - 1))])
                    sync.dma_start(out=y_o[128 * (it - 1):128 * it, :],
                                   in_=y_t[:]).then_inc(tL, 16)
                sync.dma_start(out=xh_t[:],
                               in_=xh[128 * it:128 * (it + 1), :]).then_inc(tL, 16)
                if it >= 1:
                    sync.wait_ge(ac, M[("ac", ("ab", it - 1, NS - 1, 1))])
                sync.dma_start(out=e_t[:],
                               in_=ee[128 * it:128 * (it + 1), :]).then_inc(tL, 16)
                for s_ in range(NS):
                    for hf in range(2):
                        if it > 0 or s_ > 0 or hf > 0:
                            pk = ("hc", it, s_ - 1, 1) if (hf == 0 and s_ > 0) else (
                                ("hc", it - 1, NS - 1, 1) if hf == 0 else
                                ("u", it, s_, 0))
                            sem_, = ("pl",) if hf == 0 else ("dv",)
                            sync.wait_ge(pl if hf == 0 else dv,
                                         M[(("pl" if hf == 0 else "dv"), pk)])
                        sync.dma_start(
                            out=rep_t[:],
                            in_=bc[s_:s_ + 1, hf * HF:(hf + 1) * HF]
                            .partition_broadcast(128)).then_inc(tL, 16)
                    for hf in range(2):
                        sync.wait_ge(dv, M[("dv", ("u", it, s_, 1))] if hf == 0
                                     else M[("pl", ("hc", it, s_, 0))] * 0 +
                                     M[("dv", ("u", it, s_, 1))])
                        if hf == 1:
                            sync.wait_ge(pl, M[("pl", ("hc", it, s_, 0))])
                        sync.dma_start(
                            out=rep_t[:],
                            in_=bc[16 + s_:17 + s_, hf * HF:(hf + 1) * HF]
                            .partition_broadcast(128)).then_inc(tL, 16)
            sync.wait_ge(dv, M[("dv", ("yd", NT - 1))])
            sync.dma_start(out=y_o[128 * (NT - 1):128 * NT, :],
                           in_=y_t[:]).then_inc(tL, 16)
            sync.wait_ge(tL, cnt["tL"])

        @block.gpsimd
        def _(g):
            load_consts(g, wI)
            g.dma_start(out=alog_sb[:], in_=alog[:]).then_inc(wI, 16)
            g.dma_start(out=dv_sb[:], in_=dvec[:]).then_inc(wI, 16)
            for it in range(NT):
                g.wait_ge(dv, M[("dv", ("neg", it))])
                if it >= 1:
                    g.wait_ge(ac, M[("ac", ("ab", it - 1, NS - 1, 1))])
                g.dma_start(out=an2_sb[:], in_=an_sb[:]).then_inc(aB, 16)
                for s_ in range(NS):
                    for hf in range(2):
                        g.wait_ge(dv, M[("dv", ("h", it, s_, hf))])
                        g.wait_ge(tL, M[("tL", ("c", it, s_, hf))])
                        if it > 0 or s_ > 0:
                            pk = ("y", it, s_ - 1, hf) if s_ > 0 else \
                                 ("y", it - 1, NS - 1, hf)
                            g.wait_ge(dv, M[("dv", pk)])
                        g.tensor_tensor(out=uc[hf][:],
                                        in0=h_t[:, hf * HF:(hf + 1) * HF],
                                        in1=rep_t[:],
                                        op=ALU.mult).then_inc(pl, 1)

        @block.scalar
        def _(s):
            s.wait_ge(wI, 48)
            for it in range(NT):
                s.wait_ge(tL, M[("tL", ("ee", it))])
                if it >= 1:
                    s.wait_ge(dv, M[("dv", ("neg", it - 1))])
                s.activation(out=an_raw[:],
                             in_=alog_sb[:, it * 16:(it + 1) * 16], func=AF.Exp)
                s.activation(out=spac[0:1, 0:8], in_=cs[0:1, 0:8],
                             func=AF.Copy).then_inc(ac, 1)
                for s_ in range(NS):
                    for hf in range(2):
                        if s_ == 0 and hf == 0:
                            s.wait_ge(aB, M[("aB", ("aneg", it))])
                        idx = s_ * 2 + hf
                        if it > 0 or idx >= 1:
                            pidx = idx - 1
                            pk = ("h", it, pidx // 2, pidx % 2) if idx >= 1 \
                                else ("h", it - 1, NS - 1, 1)
                            s.wait_ge(dv, M[("dv", pk)])
                        s.activation(out=ab_t[hf][:],
                                     in_=e_t[:, hf * HF:(hf + 1) * HF],
                                     func=AF.Exp,
                                     bias=an2_sb[:, s_:s_ + 1],
                                     scale=an2_sb[:, s_:s_ + 1]).then_inc(ac, 1)

        @block.vector
        def _(v):
            for it in range(NT):
                v.wait_ge(ac, M[("ac", ("expan", it))])
                v.tensor_scalar(out=an_sb[:], in0=an_raw[:], scalar1=-1.0,
                                scalar2=None, op0=ALU.mult).then_inc(dv, 1)
                v.wait_ge(tL, M[("tL", ("ee", it))])
                v.wait_ge(tL, M[("tL", ("xh", it))])
                for s_ in range(NS):
                    for hf in range(2):
                        v.wait_ge(tL, M[("tL", ("b", it, s_, hf))])
                        v.scalar_tensor_tensor(out=uc[hf][:],
                                               in0=e_t[:, hf * HF:(hf + 1) * HF],
                                               scalar=1.0,
                                               in1=xh_t[:, hf * HF:(hf + 1) * HF],
                                               op0=ALU.add, op1=ALU.mult)
                        v.tensor_tensor(out=uc[hf][:], in0=uc[hf][:],
                                        in1=rep_t[:],
                                        op=ALU.mult).then_inc(dv, 1)
                        v.wait_ge(ac, M[("ac", ("ab", it, s_, hf))])
                        init = 0.0 if hf == 0 else h_t[:, HF - 1:HF]
                        v.tensor_tensor_scan(out=h_t[:, hf * HF:(hf + 1) * HF],
                                             data0=ab_t[hf][:], data1=uc[hf][:],
                                             initial=init, op0=ALU.mult,
                                             op1=ALU.add).then_inc(dv, 1)
                    for hf in range(2):
                        v.wait_ge(pl, M[("pl", ("hc", it, s_, hf))])
                        if it > 0 and s_ == 0 and hf == 0:
                            v.wait_ge(tL, M[("tL", ("yst", it - 1))])
                        if s_ == 0:
                            v.tensor_copy(y_t[:, hf * HF:(hf + 1) * HF],
                                          uc[hf][:]).then_inc(dv, 1)
                        else:
                            v.tensor_add(y_t[:, hf * HF:(hf + 1) * HF],
                                         y_t[:, hf * HF:(hf + 1) * HF],
                                         uc[hf][:]).then_inc(dv, 1)
                v.scalar_tensor_tensor(out=y_t[:], in0=xh_t[:],
                                       scalar=dv_sb[:, it:it + 1], in1=y_t[:],
                                       op0=ALU.mult,
                                       op1=ALU.add).then_inc(dv, 1)
    return nc


def run_l1scan(res_pre, A_log, Dv):
    nc = build_l1scan()
    in_maps = []
    for c in range(NCORES):
        b, ih = c // 2, c % 2
        c0, c1 = 2 * b, 2 * b + 1
        r0, r1 = res_pre.results[c0], res_pre.results[c1]
        xh_full = np.concatenate([r0["xhg_o"], r1["xhg_o"]], axis=1)
        ee_full = np.concatenate([r0["e_o"], r1["e_o"]], axis=1)
        bc_full = np.concatenate([r0["bc_o"], r1["bc_o"]], axis=1)
        sl = slice(ih * 1024, (ih + 1) * 1024)
        al = A_log[sl].reshape(8, 128, 16).transpose(1, 0, 2).reshape(128, 128)
        dvp = Dv[sl].reshape(8, 128).T
        in_maps.append({
            "xh": np.ascontiguousarray(xh_full[sl]),
            "ee": np.ascontiguousarray(ee_full[sl]),
            "bc": np.ascontiguousarray(bc_full),
            "alog": np.ascontiguousarray(al).astype(np.float32),
            "dvec": np.ascontiguousarray(dvp).astype(np.float32),
            "consts": CONSTS_NP,
        })
    res = run_bass_kernel_spmd(nc, in_maps, core_ids=list(range(NCORES)))
    return res



def build_l1out():
    nc = new_nc()
    yT = nc.dram_tensor("yT", [2048, ROWS], F32, kind="ExternalInput").ap()
    x2b = nc.dram_tensor("x2b", [ROWS, D], BF16, kind="ExternalInput").ap()
    wo = nc.dram_tensor("wo", [16, 128, D], F32, kind="ExternalInput").ap()
    ident = nc.dram_tensor("ident", [128, 128], BF16, kind="ExternalInput").ap()
    x3 = nc.dram_tensor("x3", [ROWS, D], F32, kind="ExternalOutput").ap()
    x3b = nc.dram_tensor("x3b", [ROWS, D], BF16, kind="ExternalOutput").ap()

    from contextlib import ExitStack
    with ExitStack() as ctx:
        e_ = ctx.enter_context
        cd, cs, load_consts = decl_consts(nc, e_)
        wo_sb = e_(nc.sbuf_tensor("wo_sb", [128, 16 * D], BF16))
        id_sb = e_(nc.sbuf_tensor("id_sb", [128, 128], BF16))
        ybT = e_(nc.sbuf_tensor("ybT", [128, 16 * ROWS], BF16))
        x_bf = [e_(nc.sbuf_tensor(f"x_bf{j}", [128, D], BF16)) for j in range(2)]
        x3_f = [e_(nc.sbuf_tensor(f"x3_f{j}", [128, D], F32)) for j in range(2)]
        sq_f = e_(nc.sbuf_tensor("sq_f", [128, D], BF16))
        stats = [e_(nc.sbuf_tensor(f"stats{j}", [128, 4], F32)) for j in range(2)]
        stats2 = [e_(nc.sbuf_tensor(f"stats2{j}", [128, 4], F32)) for j in range(2)]
        ps_o = [e_(nc.psum_tensor(f"ps_o{j}", [128, D], F32)) for j in range(2)]

        ld_y = e_(nc.semaphore("ld_y"))
        ld_x = e_(nc.semaphore("ld_x"))
        wS = e_(nc.semaphore("wS"))
        wl = e_(nc.semaphore("wl"))
        pe = e_(nc.semaphore("pe"))
        dv = e_(nc.semaphore("dv"))
        sc = e_(nc.semaphore("sc"))
        sb = e_(nc.semaphore("sb"))
        st = e_(nc.semaphore("st"))
        block = e_(nc.Block())

        @block.sync
        def _(sync):
            sync.dma_start(out=id_sb[:], in_=ident[:]).then_inc(wS, 16)
            for i in range(NCH):
                if i >= 2:
                    sync.wait_ge(pe, i - 1)
                sync.dma_start(out=x_bf[i % 2][:],
                               in_=x2b[128 * i:128 * (i + 1), :]).then_inc(ld_x, 16)

        @block.gpsimd
        def _(g):
            for jc in range(16):
                g.dma_start(out=ybT[:, jc * ROWS:(jc + 1) * ROWS],
                            in_=yT[128 * jc:128 * (jc + 1), :]).then_inc(ld_y, 16)
            for jc in range(16):
                g.dma_start(out=wo_sb[:, jc * D:(jc + 1) * D],
                            in_=wo[jc]).then_inc(wl, 16)
            load_consts(g, wl)
            for i in range(NCH):
                g.wait_ge(sc, i + 1)
                g.dma_start(out=stats2[i % 2][:],
                            in_=stats[i % 2][:]).then_inc(sb, 16)
                g.wait_ge(dv, i + 1)
                g.dma_start(out=x3[128 * i:128 * (i + 1), :],
                            in_=x3_f[i % 2][:]).then_inc(st, 16)
                g.dma_start(out=x3b[128 * i:128 * (i + 1), :],
                            in_=x3_f[i % 2][:]).then_inc(st, 16)
            g.wait_ge(st, 32 * NCH)

        @block.tensor
        def _(t):
            t.wait_ge(wl, 17 * 16)
            t.wait_ge(wS, 16)
            t.wait_ge(ld_y, 16 * 16)
            for i in range(NCH):
                j = i % 2
                t.wait_ge(ld_x, 16 * (i + 1))
                if i >= 2:
                    t.wait_ge(dv, i - 1)
                for n in range(2):
                    for k in range(16):
                        t.matmul(ps_o[j][:, 512 * n:512 * (n + 1)],
                                 ybT[:, k * ROWS + 128 * i:k * ROWS + 128 * (i + 1)],
                                 wo_sb[:, k * D + 512 * n:k * D + 512 * (n + 1)],
                                 start=(k == 0), stop=False)
                    t.matmul(ps_o[j][:, 512 * n:512 * (n + 1)],
                             id_sb[:], x_bf[j][:, 512 * n:512 * (n + 1)],
                             start=False, stop=True)
                t.drain().then_inc(pe, 1)

        @block.scalar
        def _(s):
            s.wait_ge(wl, 17 * 16)
            for i in range(NCH):
                j = i % 2
                s.wait_ge(pe, i + 1)
                if i >= 2:
                    s.wait_ge(sb, 16 * (i - 1))
                emit_ln_stats(s, ps_o[j][:], sq_f[:], stats[j]).then_inc(sc, 1)

        @block.vector
        def _(v):
            for i in range(NCH):
                j = i % 2
                v.wait_ge(sb, 16 * (i + 1))
                if i >= 2:
                    v.wait_ge(st, 32 * (i - 1))
                v.tensor_scalar(out=x3_f[j][:], in0=ps_o[j][:],
                                scalar1=stats2[j][:, 1:2],
                                scalar2=stats2[j][:, 3:4],
                                op0=ALU.add, op1=ALU.mult).then_inc(dv, 1)
    return nc



NHC1 = 16


def build_l1ffn():
    nc = new_nc()
    x3b = nc.dram_tensor("x3b", [ROWS, D], BF16, kind="ExternalInput").ap()
    wg = nc.dram_tensor("wg", [D, 8], F32, kind="ExternalInput").ap()
    bg = nc.dram_tensor("bg", [1, 8], F32, kind="ExternalInput").ap()
    w1 = nc.dram_tensor("w1", [NE, NHC1, 128, 1024], F32, kind="ExternalInput").ap()
    b1 = nc.dram_tensor("b1", [NE, 128, NHC1], F32, kind="ExternalInput").ap()
    w2 = nc.dram_tensor("w2", [NE, NHC1, 128, 1024], F32, kind="ExternalInput").ap()
    b2 = nc.dram_tensor("b2", [NE, 1, 1024], F32, kind="ExternalInput").ap()
    ident = nc.dram_tensor("ident", [128, 128], BF16, kind="ExternalInput").ap()
    out = nc.dram_tensor("out", [ROWS, D], F32, kind="ExternalOutput").ap()

    PE_ST = 8 + NE * (NHC1 + NSUB) + NSUB

    def pe_router(st, ch): return st * PE_ST + ch + 1
    def pe_hc(st, e, hc): return st * PE_ST + 8 + e * (NHC1 + NSUB) + hc + 1
    def pe_sub(st, e, sub):
        return st * PE_ST + 8 + e * (NHC1 + NSUB) + NHC1 + sub + 1
    def pe_ln(st, sub):
        return st * PE_ST + 8 + NE * (NHC1 + NSUB) + sub + 1
    AC_ST = 16 + NE * NHC1 + NSUB

    def ac_router(st, ch): return st * AC_ST + 2 * ch + 1
    def ac_rcp(st, ch): return st * AC_ST + 2 * ch + 2
    def ac_h(st, e, hc): return st * AC_ST + 16 + e * NHC1 + hc + 1
    def ac_ln(st, sub): return st * AC_ST + 16 + NE * NHC1 + sub + 1
    DV_ST = 16 + NE * NSUB + NSUB

    def dv_s12(st, ch): return 1 + st * DV_ST + 2 * ch + 1
    def dv_coef(st, ch): return 1 + st * DV_ST + 2 * ch + 2
    def dv_stt(st, e, sub):
        return 1 + st * DV_ST + 16 + e * NSUB + sub + 1
    def dv_x2(st, sub):
        return 1 + st * DV_ST + 16 + NE * NSUB + sub + 1

    def wU_done(st, e, hc): return 16 * ((st * NE + e) * NHC1 + hc + 1)
    def wD_done(st, e, hc): return 16 * ((st * NE + e) * NHC1 + hc + 1)
    def rB_done(st, ch): return 16 * (st * 8 + ch + 1)
    def gB_done(st, ch): return 16 * (st * 8 + ch + 1)
    def cB_done(st, ch): return 16 * (st * 8 + ch + 1)
    def sB_done(st, sub): return 16 * (st * NSUB + sub + 1)
    def stO_done(st, sub): return 16 * (st * NSUB + sub + 1)

    from contextlib import ExitStack
    with ExitStack() as ctx:
        e_ = ctx.enter_context
        cd, cs, load_consts = decl_consts(nc, e_)
        id_sb = e_(nc.sbuf_tensor("id_sb", [128, 128], BF16))
        wg_sb = e_(nc.sbuf_tensor("wg_sb", [128, 8 * 8], BF16))
        bg_sb = e_(nc.sbuf_tensor("bg_sb", [1, 8], BF16))
        ones1 = e_(nc.sbuf_tensor("ones1", [1, 128], BF16))
        x3T = e_(nc.sbuf_tensor("x3T", [128, 8 * TST], BF16))
        hT = e_(nc.sbuf_tensor("hT", [128, NHC1 * TST], BF16))
        wd_sb = e_(nc.sbuf_tensor("wd_sb", [128, NHC1 * D], BF16))
        wu_b = [e_(nc.sbuf_tensor(f"wu_b{j}", [128, D], BF16)) for j in range(2)]
        b1_sb = [e_(nc.sbuf_tensor(f"b1_sb{j}", [128, NHC1], F32)) for j in range(2)]
        b2_sb = [e_(nc.sbuf_tensor(f"b2_sb{j}", [1, D], BF16)) for j in range(2)]
        acc = e_(nc.sbuf_tensor("acc", [128, NSUB * D], BF16))
        x2f = [e_(nc.sbuf_tensor(f"x2f{j}", [128, D], F32)) for j in range(2)]
        sq_f = e_(nc.sbuf_tensor("sq_f", [128, D], BF16))
        coef = e_(nc.sbuf_tensor("coef", [128, 64], F32))
        rstat = [e_(nc.sbuf_tensor(f"rstat{j}", [128, 9], F32)) for j in range(2)]
        rstat2 = [e_(nc.sbuf_tensor(f"rstat2{j}", [128, 9], F32)) for j in range(2)]
        s12 = [e_(nc.sbuf_tensor(f"s12{j}", [128, 1], F32)) for j in range(2)]
        s12b = [e_(nc.sbuf_tensor(f"s12b{j}", [128, 1], F32)) for j in range(2)]
        rcp = [e_(nc.sbuf_tensor(f"rcp{j}", [128, 1], F32)) for j in range(2)]
        rcpb = [e_(nc.sbuf_tensor(f"rcpb{j}", [128, 1], F32)) for j in range(2)]
        stats = [e_(nc.sbuf_tensor(f"stats{j}", [128, 4], F32)) for j in range(2)]
        stats2 = [e_(nc.sbuf_tensor(f"stats2{j}", [128, 4], F32)) for j in range(2)]
        mcol = e_(nc.sbuf_tensor("mcol", [128, 4], F32))
        mask8 = e_(nc.sbuf_tensor("mask8", [128, 8], F32))
        num8 = e_(nc.sbuf_tensor("num8", [128, 8], F32))
        spac = e_(nc.sbuf_tensor("spac", [1, len(CONST_VALS)], F32))
        psA = [e_(nc.psum_tensor(f"psA{j}", [128, D], F32)) for j in range(2)]
        ps_d = e_(nc.psum_tensor("ps_d", [128, D], F32))

        wI = e_(nc.semaphore("wI"))
        wS = e_(nc.semaphore("wS"))
        tD = e_(nc.semaphore("tD"))
        aL = e_(nc.semaphore("aL"))
        wU = e_(nc.semaphore("wU"))
        wD = e_(nc.semaphore("wD"))
        rB = e_(nc.semaphore("rB"))
        gB = e_(nc.semaphore("gB"))
        cB = e_(nc.semaphore("cB"))
        sB = e_(nc.semaphore("sB"))
        pe = e_(nc.semaphore("pe"))
        ac = e_(nc.semaphore("ac"))
        dv = e_(nc.semaphore("dv"))
        stO = e_(nc.semaphore("stO"))
        block = e_(nc.Block())

        @block.sync
        def _(sync):
            sync.dma_start(out=id_sb[:], in_=ident[:]).then_inc(wS, 16)
            for st in range(NST):
                if st >= 1:
                    sync.wait_ge(pe, pe_hc(st - 1, 7, NHC1 - 1))
                for c in range(8):
                    sync.dma_start(
                        out=x3T[:, c * TST:(c + 1) * TST],
                        in_=x3b[st * TST:(st + 1) * TST, c * 128:(c + 1) * 128],
                        transpose=True).then_inc(tD, 16)

        @block.gpsimd
        def _(g):
            g.dma_start(out=wg_sb.ap().rearrange("p (c n) -> p c n", c=8),
                        in_=wg.rearrange("(c p) n -> p c n", p=128)).then_inc(wI, 16)
            g.dma_start(out=bg_sb[:], in_=bg[:]).then_inc(wI, 16)
            load_consts(g, wI)
            for st in range(NST):
                for sub in range(NSUB):
                    if st >= 1:
                        g.wait_ge(pe, pe_ln(st - 1, sub))
                    g.dma_start(out=acc[:, sub * D:(sub + 1) * D],
                                in_=x3b[st * TST + sub * 128:
                                        st * TST + (sub + 1) * 128, :]
                                ).then_inc(aL, 16)
                for ch in range(8):
                    g.wait_ge(ac, ac_router(st, ch))
                    gl = st * 8 + ch
                    if gl >= 2:
                        g.wait_ge(dv, dv_coef((gl - 2) // 8, (gl - 2) % 8))
                    g.dma_start(out=rstat2[ch % 2][:],
                                in_=rstat[ch % 2][:]).then_inc(rB, 16)
                    g.wait_ge(dv, dv_s12(st, ch))
                    if gl >= 2:
                        g.wait_ge(ac, ac_rcp((gl - 2) // 8, (gl - 2) % 8))
                    g.dma_start(out=s12b[ch % 2][:],
                                in_=s12[ch % 2][:]).then_inc(gB, 16)
                    g.wait_ge(ac, ac_rcp(st, ch))
                    if gl >= 2:
                        g.wait_ge(dv, dv_coef((gl - 2) // 8, (gl - 2) % 8))
                    g.dma_start(out=rcpb[ch % 2][:],
                                in_=rcp[ch % 2][:]).then_inc(cB, 16)
                for e in range(NE):
                    for hc in range(NHC1):
                        lin = e * NHC1 + hc
                        if st > 0 or lin >= 2:
                            pl_ = lin - 2
                            g.wait_ge(pe, pe_hc(st, pl_ // NHC1, pl_ % NHC1)
                                      if pl_ >= 0 else pe_hc(st - 1, 7, NHC1 + pl_))
                        g.dma_start(out=wu_b[hc % 2][:],
                                    in_=w1[e, hc]).then_inc(wU, 16)
                        if hc == 0 and (st > 0 or e > 0):
                            g.wait_ge(pe, pe_sub(st, e - 1, NSUB - 1) if e > 0
                                      else pe_sub(st - 1, 7, NSUB - 1))
                            g.dma_start(out=b1_sb[e % 2][:],
                                        in_=b1[e]).then_inc(wD, 16)
                            g.dma_start(out=b2_sb[e % 2][:],
                                        in_=b2[e]).then_inc(wD, 16)
                        elif hc == 0:
                            g.dma_start(out=b1_sb[e % 2][:],
                                        in_=b1[e]).then_inc(wD, 16)
                            g.dma_start(out=b2_sb[e % 2][:],
                                        in_=b2[e]).then_inc(wD, 16)
                        g.dma_start(out=wd_sb[:, hc * D:(hc + 1) * D],
                                    in_=w2[e, hc]).then_inc(wD, 16)
                for sub in range(NSUB):
                    g.wait_ge(ac, ac_ln(st, sub))
                    g.dma_start(out=stats2[sub % 2][:],
                                in_=stats[sub % 2][:]).then_inc(sB, 16)
                    g.wait_ge(dv, dv_x2(st, sub))
                    r0 = st * TST + sub * 128
                    g.dma_start(out=out[r0:r0 + 128, :],
                                in_=x2f[sub % 2][:]).then_inc(stO, 16)
            g.wait_ge(stO, 16 * NST * NSUB)

        @block.tensor
        def _(t):
            t.wait_ge(wI, 48)
            t.wait_ge(wS, 16)
            t.wait_ge(dv, 1)
            for st in range(NST):
                t.wait_ge(tD, 128 * (st + 1))
                t.wait_ge(aL, 128 * (st + 1))
                for ch in range(8):
                    if st > 0 and ch == 0:
                        t.wait_ge(dv, dv_x2(st - 1, NSUB - 1))
                    if ch >= 1:
                        t.wait_ge(ac, ac_router(st, ch - 1))
                    for c in range(8):
                        t.matmul(ps_d[0:128, 0:8],
                                 x3T[:, c * TST + ch * 128:c * TST + (ch + 1) * 128],
                                 wg_sb[:, c * 8:(c + 1) * 8],
                                 start=(c == 0), stop=False)
                    t.matmul(ps_d[0:128, 0:8], ones1[:], bg_sb[:],
                             start=False, stop=True)
                    t.drain().then_inc(pe, 1)
                for e in range(NE):
                    for hc in range(NHC1):
                        lin = e * NHC1 + hc
                        t.wait_ge(wU, wU_done(st, e, hc))
                        t.wait_ge(wD, 16 * ((st * NE + e) * (NHC1 + 2) + 2))
                        if st > 0 or lin >= 2:
                            pl_ = lin - 2
                            t.wait_ge(ac, ac_h(st, pl_ // NHC1, pl_ % NHC1)
                                      if pl_ >= 0 else ac_h(st - 1, 7, NHC1 + pl_))
                        for tb in range(2):
                            for c in range(8):
                                t.matmul(psA[hc % 2][:, tb * 512:(tb + 1) * 512],
                                         wu_b[hc % 2][:, c * 128:(c + 1) * 128],
                                         x3T[:, c * TST + tb * 512:c * TST + (tb + 1) * 512],
                                         start=(c == 0), stop=(c == 7))
                        t.drain().then_inc(pe, 1)
                    t.wait_ge(ac, ac_h(st, e, NHC1 - 1))
                    for sub in range(NSUB):
                        if sub > 0:
                            t.wait_ge(dv, dv_stt(st, e, sub - 1))
                        elif e > 0:
                            t.wait_ge(dv, dv_stt(st, e - 1, NSUB - 1))
                        elif st > 0:
                            t.wait_ge(dv, dv_x2(st - 1, NSUB - 1))
                        if e == 0 and sub == 0:
                            t.wait_ge(ac, ac_router(st, 7))
                        for n in range(2):
                            for hk in range(NHC1):
                                t.matmul(ps_d[:, n * 512:(n + 1) * 512],
                                         hT[:, hk * TST + sub * 128:
                                            hk * TST + (sub + 1) * 128],
                                         wd_sb[:, hk * D + n * 512:hk * D + (n + 1) * 512],
                                         start=(hk == 0), stop=False)
                            t.matmul(ps_d[:, n * 512:(n + 1) * 512],
                                     ones1[:], b2_sb[e % 2][0:1, n * 512:(n + 1) * 512],
                                     start=False, stop=True)
                        t.drain().then_inc(pe, 1)
                for sub in range(NSUB):
                    t.wait_ge(dv, dv_stt(st, 7, NSUB - 1))
                    if sub > 0:
                        t.wait_ge(ac, ac_ln(st, sub - 1))
                        t.wait_ge(dv, dv_x2(st, sub - 1))
                    for n in range(2):
                        t.matmul(ps_d[:, n * 512:(n + 1) * 512],
                                 id_sb[:], acc[:, sub * D + n * 512:sub * D + (n + 1) * 512],
                                 start=True, stop=True)
                    t.drain().then_inc(pe, 1)

        @block.scalar
        def _(s):
            s.wait_ge(wI, 48)
            for st in range(NST):
                for ch in range(8):
                    s.wait_ge(pe, pe_router(st, ch))
                    gl = st * 8 + ch
                    if gl >= 2:
                        s.wait_ge(rB, rB_done((gl - 2) // 8, (gl - 2) % 8))
                    s.activation(out=spac[0:1, 0:8], in_=cs[0:1, 0:8],
                                 func=AF.Copy)
                    s.activation(out=rstat[ch % 2][:, 0:8], in_=ps_d[0:128, 0:8],
                                 func=AF.Exp,
                                 accum_out=rstat[ch % 2][:, 8:9]).then_inc(ac, 1)
                    s.wait_ge(gB, gB_done(st, ch))
                    if gl >= 2:
                        s.wait_ge(cB, cB_done((gl - 2) // 8, (gl - 2) % 8))
                    s.activation(out=rcp[ch % 2][:], in_=s12b[ch % 2][:],
                                 func=AF.Ln)
                    s.activation(out=rcp[ch % 2][:], in_=rcp[ch % 2][:],
                                 func=AF.Exp, scale=-1.0)
                    s.activation(out=spac[0:1, 0:8], in_=cs[0:1, 0:8],
                                 func=AF.Copy).then_inc(ac, 1)
                for e in range(NE):
                    for hc in range(NHC1):
                        lin = e * NHC1 + hc
                        s.wait_ge(pe, pe_hc(st, e, hc))
                        s.activation(out=hT.ap().rearrange("p (h n) -> p h n",
                                                           h=NHC1)[:, hc, :],
                                     in_=psA[hc % 2][:], func=AF.Relu,
                                     bias=b1_sb[e % 2][:, hc:hc + 1]).then_inc(ac, 1)
                for sub in range(NSUB):
                    s.wait_ge(pe, pe_ln(st, sub))
                    if st * NSUB + sub >= 2:
                        ps_ = st * NSUB + sub - 2
                        s.wait_ge(sB, sB_done(ps_ // NSUB, ps_ % NSUB))
                    emit_ln_stats(s, ps_d[:], sq_f[:],
                                  stats[sub % 2]).then_inc(ac, 1)

        @block.vector
        def _(v):
            v.memset(ones1[:], 1.0).then_inc(dv, 1)
            for st in range(NST):
                for ch in range(8):
                    v.wait_ge(rB, rB_done(st, ch))
                    E8 = rstat2[ch % 2][:, 0:8]
                    v.reduce_max(mcol[:, 0:1], E8, axis=AX.X)
                    v.tensor_copy(spac[:], cs[0:1, :])
                    v.tensor_copy(spac[:], cs[0:1, :])
                    v.tensor_tensor(out=mask8[:], in0=E8,
                                    in1=mcol[:, 0:1].broadcast_to([128, 8]),
                                    op=ALU.is_equal)
                    v.tensor_scalar(out=num8[:], in0=mask8[:], scalar1=-1.0,
                                    scalar2=1.0, op0=ALU.mult, op1=ALU.add)
                    v.tensor_mul(num8[:], num8[:], E8)
                    v.reduce_max(mcol[:, 1:2], num8[:], axis=AX.X)
                    v.tensor_copy(spac[:], cs[0:1, :])
                    v.tensor_copy(spac[:], cs[0:1, :])
                    v.tensor_tensor(out=num8[:], in0=num8[:],
                                    in1=mcol[:, 1:2].broadcast_to([128, 8]),
                                    op=ALU.is_equal)
                    v.tensor_add(num8[:], num8[:], mask8[:])
                    v.tensor_mul(num8[:], num8[:], E8)
                    v.tensor_tensor(out=s12[ch % 2][:], in0=mcol[:, 0:1],
                                    in1=mcol[:, 1:2], op=ALU.add)
                    v.tensor_copy(spac[:], cs[0:1, :]).then_inc(dv, 1)
                    v.wait_ge(cB, cB_done(st, ch))
                    v.tensor_tensor(out=coef[:, ch * 8:(ch + 1) * 8],
                                    in0=num8[:],
                                    in1=rcpb[ch % 2][:, 0:1].broadcast_to([128, 8]),
                                    op=ALU.mult).then_inc(dv, 1)
                for e in range(NE):
                    for sub in range(NSUB):
                        v.wait_ge(pe, pe_sub(st, e, sub))
                        v.scalar_tensor_tensor(
                            out=acc[:, sub * D:(sub + 1) * D], in0=ps_d[:],
                            scalar=coef[:, (sub * 8 + e):(sub * 8 + e) + 1],
                            in1=acc[:, sub * D:(sub + 1) * D],
                            op0=ALU.mult, op1=ALU.add).then_inc(dv, 1)
                for sub in range(NSUB):
                    v.wait_ge(sB, sB_done(st, sub))
                    if st * NSUB + sub >= 2:
                        pv_ = st * NSUB + sub - 2
                        v.wait_ge(stO, stO_done(pv_ // NSUB, pv_ % NSUB))
                    v.tensor_scalar(out=x2f[sub % 2][:], in0=ps_d[:],
                                    scalar1=stats2[sub % 2][:, 1:2],
                                    scalar2=stats2[sub % 2][:, 3:4],
                                    op0=ALU.add, op1=ALU.mult).then_inc(dv, 1)
    return nc



def run_l0ffn(x1_parts, x1b_parts, w_router, w_up, w_gate, w_down,
              ws_up, ws_gate, ws_down):
    nc = build_moe('l0')
    wu = np.concatenate([np.stack([prep_upgate(w_up[e]) for e in range(7)]),
                         prep_upgate(ws_up)[None]]).astype(np.float32)
    wgp = np.concatenate([np.stack([prep_upgate(w_gate[e]) for e in range(7)]),
                          prep_upgate(ws_gate)[None]]).astype(np.float32)
    wd = np.concatenate([np.stack([w_down[e].reshape(32, 128, 1024)
                                   for e in range(7)]),
                         ws_down.reshape(32, 128, 1024)[None]]).astype(np.float32)
    in_maps = [{"xb": x1b_parts[c],
                "wr": np.ascontiguousarray(w_router), "wu": wu, "wg": wgp,
                "wd": wd, "ident": IDENT, "consts": CONSTS_NP}
               for c in range(NCORES)]
    res = run_bass_kernel_spmd(nc, in_maps, core_ids=list(range(NCORES)))
    return ([res.results[c]["out_f"] for c in range(NCORES)],
            [res.results[c]["out_b"] for c in range(NCORES)])


def run_l1out(res_scan, x2b_parts, out_w):
    nc = build_l1out()
    wo_prep = np.ascontiguousarray(out_w.T.reshape(16, 128, D)).astype(np.float32)
    in_maps = []
    for c in range(NCORES):
        b, half = c // 2, c % 2
        y0 = res_scan.results[2 * b]["y_o"][:, half * ROWS:(half + 1) * ROWS]
        y1 = res_scan.results[2 * b + 1]["y_o"][:, half * ROWS:(half + 1) * ROWS]
        yT = np.ascontiguousarray(np.concatenate([y0, y1], axis=0))
        in_maps.append({"yT": yT, "x2b": x2b_parts[c], "wo": wo_prep,
                        "ident": IDENT, "consts": CONSTS_NP})
    res = run_bass_kernel_spmd(nc, in_maps, core_ids=list(range(NCORES)))
    return ([res.results[c]["x3"] for c in range(NCORES)],
            [res.results[c]["x3b"] for c in range(NCORES)])


def run_l1ffn(x3b_parts, Wg, bg, W1, b1, W2, b2):
    nc = build_moe('l1')
    w1p = np.stack([prep_upgate(W1[e]) for e in range(NE)]).astype(np.float32)
    w2p = np.stack([W2[e].reshape(16, 128, D) for e in range(NE)]).astype(np.float32)
    b1p = np.stack([b1[e].reshape(16, 128).T for e in range(NE)]).astype(np.float32)
    b2p = np.ascontiguousarray(b2.reshape(NE, 1, D)).astype(np.float32)
    in_maps = [{"xb": x3b_parts[c], "wr": np.ascontiguousarray(Wg),
                "bg": np.ascontiguousarray(bg.reshape(1, 8)), "wu": w1p,
                "b1": b1p, "wd": w2p, "b2": b2p, "ident": IDENT,
                "consts": CONSTS_NP} for c in range(NCORES)]
    res = run_bass_kernel_spmd(nc, in_maps, core_ids=list(range(NCORES)))
    return [res.results[c]["out_f"] for c in range(NCORES)]


def kernel(**inp):
    x = np.ascontiguousarray(np.asarray(inp["L0_attn_hidden_states"],
                                        dtype=np.float32))
    attn = run_l0attn(x, np.asarray(inp["L0_attn_Wq"], np.float32),
                      np.asarray(inp["L0_attn_Wk"], np.float32),
                      np.asarray(inp["L0_attn_Wv"], np.float32),
                      np.asarray(inp["L0_attn_proj"], np.float32))
    x1, x1b_parts = run_l0out(attn, x, np.asarray(inp["L0_attn_Wo"], np.float32),
                              np.asarray(inp["L0_attn_bo"], np.float32))
    x1_parts = shard_rows(x1)
    x2_parts, x2b_parts = run_l0ffn(
        x1_parts, x1b_parts,
        np.asarray(inp["L0_ffn_w_router"], np.float32),
        np.asarray(inp["L0_ffn_w_up"], np.float32),
        np.asarray(inp["L0_ffn_w_gate"], np.float32),
        np.asarray(inp["L0_ffn_w_down"], np.float32),
        np.asarray(inp["L0_ffn_ws_up"], np.float32),
        np.asarray(inp["L0_ffn_ws_gate"], np.float32),
        np.asarray(inp["L0_ffn_ws_down"], np.float32))
    x2_full = unshard_rows(x2_parts)
    res_pre = run_l1pre(x2_full, x2b_parts,
                        np.asarray(inp["L1_attn_in_w"], np.float32),
                        np.asarray(inp["L1_attn_in_b"], np.float32),
                        np.asarray(inp["L1_attn_conv_w"], np.float32),
                        np.asarray(inp["L1_attn_conv_b"], np.float32),
                        np.asarray(inp["L1_attn_xproj_w"], np.float32),
                        np.asarray(inp["L1_attn_dt_w"], np.float32),
                        np.asarray(inp["L1_attn_dt_b"], np.float32))
    res_scan = run_l1scan(res_pre, np.asarray(inp["L1_attn_A_log"], np.float32),
                          np.asarray(inp["L1_attn_D"], np.float32))
    x3_parts, x3b_parts = run_l1out(res_scan, x2b_parts,
                                    np.asarray(inp["L1_attn_out_w"], np.float32))
    out_parts = run_l1ffn(x3b_parts, np.asarray(inp["L1_ffn_Wg"], np.float32),
                          np.asarray(inp["L1_ffn_bg"], np.float32),
                          np.asarray(inp["L1_ffn_W1"], np.float32),
                          np.asarray(inp["L1_ffn_b1"], np.float32),
                          np.asarray(inp["L1_ffn_W2"], np.float32),
                          np.asarray(inp["L1_ffn_b2"], np.float32))
    return unshard_rows(out_parts)
